# version 07
# baseline (speedup 1.0000x reference)
import numpy as np
import ml_dtypes
from contextlib import ExitStack

import concourse.bass as bass
import concourse.mybir as mybir
from concourse.bass_utils import run_bass_kernel_spmd

F32 = mybir.dt.float32
BF16 = mybir.dt.bfloat16
AF = mybir.ActivationFunctionType
ALU = mybir.AluOpType
AX = mybir.AxisListType
EN = mybir.EngineType

T = 8192
M = 2048
NH = 16
D = 128
NE = 8
HID = 1024
NCORE = 8
CH = 512
SUB = 512
ATTN_MULT = float(np.float32((1.0 / 128.0) ** 0.5))

BOUNDS = [1664, 3072, 4224, 5248, 6144, 6912, 7552, 8192]
NMAX = 1664

ALL_EN = [EN.PE, EN.DVE, EN.Activation, EN.Pool, EN.SP]

MODE_M1 = False
DBG = False
SKIP_ROUTING = False
SKIP_QPROJ = False
ONLY_PHASES = None


def _chunks(t0, t1, step):
    out = []
    t = t0
    while t < t1:
        w = min(step, t1 - t)
        out.append((t, w))
        t += w
    return out


class BState:
    def __init__(self, nc, ctx):
        self.nc = nc
        self.pe, self.ve, self.sc = nc.tensor, nc.vector, nc.scalar
        self.gp, self.sy = nc.gpsimd, nc.sync
        self.engs = [self.pe, self.ve, self.sc, self.gp, self.sy]
        self.sem = {}
        for nm in ("pe", "ve", "sc", "gp", "sy", "dsy", "dgp", "phase", "dbgs"):
            self.sem[nm] = ctx.enter_context(nc.semaphore("sem_" + nm))
        self.cnt = {r: {nm: 0 for nm in self.sem} for r in range(NCORE)}
        self.rank = None
        self.nphase = 0

    def _ranks(self):
        return range(NCORE) if self.rank is None else (self.rank,)

    def tok(self, nm):
        return self.cnt[0 if self.rank is None else self.rank][nm]

    def bump(self, instr, nm, n=None):
        if n is None:
            n = 16 if nm.startswith("d") else 1
        instr.then_inc(self.sem[nm], n)
        for r in self._ranks():
            self.cnt[r][nm] += n
        return self.tok(nm)

    def dma(self, eng, out, in_, q="dsy", **kw):
        ins = eng.dma_start(out=out, in_=in_, **kw)
        return self.bump(ins, q, 16)

    def wait(self, eng, nm, val):
        if val > 0:
            eng.wait_ge(self.sem[nm], val)

    def chain(self, instr, nm):
        tok = self.bump(instr, nm)
        self.eng_of(nm).wait_ge(self.sem[nm], tok)
        return tok

    def eng_of(self, nm):
        return {"pe": self.pe, "ve": self.ve, "sc": self.sc,
                "gp": self.gp, "sy": self.sy}[nm]

    def fence(self, nm):
        eng = self.eng_of(nm)
        eng.wait_ge(self.sem[nm], self.tok(nm))
        return self.bump(eng.nop(), nm)


class War:

    def __init__(self, b):
        self.b = b
        self.d = {}

    def set(self, key, nm, val=None):
        self.d[key] = [(nm, self.b.tok(nm) if val is None else val)]

    def add(self, key, nm, val=None):
        self.d.setdefault(key, []).append((nm, self.b.tok(nm) if val is None else val))

    def guard(self, eng, key):
        for nm, val in self.d.get(key, ()):
            self.b.wait(eng, nm, val)


def host_inputs(inp):
    x = np.ascontiguousarray(np.asarray(inp["L0_attn_x"], dtype=np.float32).reshape(T, M))
    wq = np.ascontiguousarray(np.asarray(inp["L0_attn_wq"], np.float32).reshape(M, NH * D))
    wk = np.ascontiguousarray(np.asarray(inp["L0_attn_wk"], np.float32).reshape(M, NH * D))
    wv = np.ascontiguousarray(np.asarray(inp["L0_attn_wv"], np.float32).reshape(M, NH * D))
    wo = np.ascontiguousarray(np.asarray(inp["L0_attn_wo"], np.float32).reshape(NH * D, M))
    w1T = np.ascontiguousarray(np.asarray(inp["L0_ffn_w1"], np.float32).transpose(0, 2, 1))
    w3T = np.ascontiguousarray(np.asarray(inp["L0_ffn_w3"], np.float32).transpose(0, 2, 1))
    w2T = np.ascontiguousarray(np.asarray(inp["L0_ffn_w2"], np.float32).transpose(0, 2, 1))
    sw1T = np.ascontiguousarray(np.asarray(inp["L0_ffn_shared_w1"], np.float32).T)
    sw3T = np.ascontiguousarray(np.asarray(inp["L0_ffn_shared_w3"], np.float32).T)
    sw2T = np.ascontiguousarray(np.asarray(inp["L0_ffn_shared_w2"], np.float32).T)
    gateT = np.ascontiguousarray(np.asarray(inp["L0_ffn_gate"], np.float32).T)
    biasrep = np.ascontiguousarray(
        np.broadcast_to(np.asarray(inp["L0_ffn_expert_bias"], np.float32)[None, :], (128, NE)))
    pos = np.arange(T, dtype=np.float32)[:, None]
    freqs = np.power(np.float32(10000.0),
                     -np.arange(64, dtype=np.float32) / np.float32(64.0))[None, :]
    rad = (pos * freqs).astype(np.float32)
    ropec = (np.cos(rad) * np.float32(ATTN_MULT)).astype(np.float32)
    ropes = (np.sin(rad) * np.float32(ATTN_MULT)).astype(np.float32)
    tri = np.where(np.arange(128)[:, None] > np.arange(128)[None, :],
                   np.float32(-1e30), np.float32(0.0)).astype(np.float32)
    return {
        "x2d": x, "wq2": wq, "wk2": wk, "wv2": wv, "wo2": wo,
        "w1T": w1T, "w3T": w3T, "w2T": w2T,
        "sw1T": sw1T, "sw3T": sw3T, "sw2T": sw2T,
        "gateT": gateT, "biasrep": biasrep,
        "ropec": ropec, "ropes": ropes, "tri": tri,
        "idn": np.eye(128, dtype=ml_dtypes.bfloat16),
        "idnf": np.eye(128, dtype=np.float32),
    }


def declare_io(nc, nmax):
    g = {}
    g["x2d"] = nc.dram_tensor("x2d", [T, M], F32, kind="ExternalInput").ap()
    for nm in ("wq2", "wk2", "wv2", "wo2"):
        g[nm] = nc.dram_tensor(nm, [M, M], F32, kind="ExternalInput").ap()
    g["w1T"] = nc.dram_tensor("w1T", [NE, M, HID], F32, kind="ExternalInput").ap()
    g["w3T"] = nc.dram_tensor("w3T", [NE, M, HID], F32, kind="ExternalInput").ap()
    g["w2T"] = nc.dram_tensor("w2T", [NE, HID, M], F32, kind="ExternalInput").ap()
    g["sw1T"] = nc.dram_tensor("sw1T", [M, HID], F32, kind="ExternalInput").ap()
    g["sw3T"] = nc.dram_tensor("sw3T", [M, HID], F32, kind="ExternalInput").ap()
    g["sw2T"] = nc.dram_tensor("sw2T", [HID, M], F32, kind="ExternalInput").ap()
    g["gateT"] = nc.dram_tensor("gateT", [M, NE], F32, kind="ExternalInput").ap()
    g["biasrep"] = nc.dram_tensor("biasrep", [128, NE], F32, kind="ExternalInput").ap()
    g["ropec"] = nc.dram_tensor("ropec", [T, 64], F32, kind="ExternalInput").ap()
    g["ropes"] = nc.dram_tensor("ropes", [T, 64], F32, kind="ExternalInput").ap()
    g["tri"] = nc.dram_tensor("tri", [128, 128], F32, kind="ExternalInput").ap()
    g["idn"] = nc.dram_tensor("idn", [128, 128], BF16, kind="ExternalInput").ap()
    g["idnf"] = nc.dram_tensor("idnf", [128, 128], F32, kind="ExternalInput").ap()
    g["OUT"] = nc.dram_tensor("OUT", [nmax, M], F32, kind="ExternalOutput").ap()
    g["K_hbm"] = nc.dram_tensor("K_hbm", [NH, D, T], BF16, kind="Internal").ap()
    g["V_hbm"] = nc.dram_tensor("V_hbm", [NH, T, D], BF16, kind="Internal").ap()
    g["Q_hbm"] = nc.dram_tensor("Q_hbm", [NH, D, nmax], BF16, kind="Internal").ap()
    g["o_hbm"] = nc.dram_tensor("o_hbm", [NH, D, nmax], BF16, kind="Internal").ap()
    g["xnown"] = nc.dram_tensor("xnown", [M, nmax], BF16, kind="Internal").ap()
    if DBG:
        g["KDBG"] = nc.dram_tensor("KDBG", [NH, D, T], BF16, kind="ExternalOutput").ap()
        g["KSB"] = nc.dram_tensor("KSB", [128, T], BF16, kind="ExternalOutput").ap()
        g["VSB"] = nc.dram_tensor("VSB", [128, (T // 128) * 129], BF16, kind="ExternalOutput").ap()
        g["PWD"] = nc.dram_tensor("PWD", [128, 16 * 512], BF16, kind="ExternalOutput").ap()
        g["OACCD"] = nc.dram_tensor("OACCD", [128, 4 * 129], F32, kind="ExternalOutput").ap()
        g["OHD"] = nc.dram_tensor("OHD", [128, nmax], BF16, kind="ExternalOutput").ap()
        g["VDBG"] = nc.dram_tensor("VDBG", [NH, T, D], BF16, kind="ExternalOutput").ap()
        g["QDBG"] = nc.dram_tensor("QDBG", [NH, D, nmax], BF16, kind="ExternalOutput").ap()
        g["ODBG"] = nc.dram_tensor("ODBG", [NH, D, nmax], BF16, kind="ExternalOutput").ap()
        g["XNDBG"] = nc.dram_tensor("XNDBG", [M, nmax], BF16, kind="ExternalOutput").ap()
        g["SDBG"] = nc.dram_tensor("SDBG", [128, 128 * NE], F32, kind="ExternalOutput").ap()
    return g


def emit_ctx_pass(b, g, p, pers, t0g, t1g, mode, a=0):
    pe, ve, sc, sy = b.pe, b.ve, b.sc, b.sy
    war = War(b)
    nmats = 2 if mode == "kv" else 1

    for (t0, w) in _chunks(t0g, t1g, CH):
        tw = w // 128
        war.guard(sy, "rc")
        rc = b.dma(sy, p["rc"].rearrange("p (tb j) -> p tb j", j=64)[:, 0:tw, :],
                   g["ropec"][t0:t0 + w, :].rearrange("(tb p) j -> p tb j", p=128))
        war.guard(sy, "rs")
        rc = b.dma(sy, p["rs"].rearrange("p (tb j) -> p tb j", j=64)[:, 0:tw, :],
                   g["ropes"][t0:t0 + w, :].rearrange("(tb p) j -> p tb j", p=128))
        for tt in range(tw):
            war.guard(sy, "x")
            xl = b.dma(sy, p["x"][:], g["x2d"][t0 + tt * 128: t0 + (tt + 1) * 128, :])
            xt = p["x"]
            b.wait(ve, "dsy", xl)
            b.wait(sc, "dsy", xl)
            b.chain(ve.reduce_sum(p["msum"][:], xt[:], axis=AX.X), "ve")
            war.guard(sc, "sq")
            sqd = b.bump(sc.activation(p["sq"][:], xt[:], AF.Square), "sc")
            b.wait(ve, "sc", sqd)
            b.chain(ve.reduce_sum(p["sqs"][:], p["sq"][:], axis=AX.X), "ve")
            war.set("sq", "ve")
            b.chain(ve.tensor_scalar(p["mean"][:], p["msum"][:], 1.0 / M, None, ALU.mult), "ve")
            b.chain(ve.tensor_scalar(p["sqm"][:], p["sqs"][:], 1.0 / M, None, ALU.mult), "ve")
            b.chain(ve.tensor_tensor(p["m2"][:], p["mean"][:], p["mean"][:], ALU.mult), "ve")
            b.chain(ve.tensor_sub(p["var"][:], p["sqm"][:], p["m2"][:]), "ve")
            vd = b.chain(ve.tensor_scalar(p["var"][:], p["var"][:], pers["epst"][:], None,
                                          ALU.add), "ve")
            b.wait(sc, "ve", vd)
            sdd = b.bump(sc.activation(p["sd"][:], p["var"][:], AF.Sqrt), "sc")
            b.wait(ve, "sc", sdd)
            b.chain(ve.reciprocal(p["rstd"][:], p["sd"][:]), "ve")
            for _nr in range(2):
                b.chain(ve.tensor_tensor(p["m2"][:], p["rstd"][:], p["rstd"][:], ALU.mult), "ve")
                b.chain(ve.tensor_tensor(p["m2"][:], p["m2"][:], p["var"][:], ALU.mult), "ve")
                b.chain(ve.tensor_scalar(p["m2"][:], p["m2"][:], -0.5, 1.5, ALU.mult, ALU.add), "ve")
                b.chain(ve.tensor_tensor(p["rstd"][:], p["rstd"][:], p["m2"][:], ALU.mult), "ve")
            war.guard(ve, "xn_bf")
            xnb = b.chain(ve.tensor_scalar(p["xn_bf"][:], xt[:], p["mean"][:], p["rstd"][:],
                                           ALU.subtract, ALU.mult), "ve")
            if mode == "q":
                war.guard(ve, "xn_f")
                xnb = b.chain(ve.tensor_scalar(p["xn_f"][:], xt[:], p["mean"][:], p["rstd"][:],
                                               ALU.subtract, ALU.mult), "ve")
            war.set("x", "ve")
            b.wait(pe, "ve", xnb)
            war.guard(pe, "pT")
            for mc in range(16):
                ins = pe.transpose(p["pT"][:, mc * 128:(mc + 1) * 128],
                                   p["xn_bf"][:, mc * 128:(mc + 1) * 128], pers["idn"][:])
            xtt = b.bump(ins, "pe")
            war.set("xn_bf", "pe", xtt)
            b.wait(ve, "pe", xtt)
            if mode == "q":
                war.guard(ve, "xnst")
                xnr = b.bump(ve.tensor_copy(
                    p["xnst"].rearrange("p (mc q) -> p mc q", q=CH)[:, :, tt * 128:(tt + 1) * 128],
                    p["pT"].rearrange("p (mc q) -> p mc q", q=128)[:, :, :]), "ve")
            else:
                war.guard(ve, "xn_fm")
                xnr = b.bump(ve.tensor_copy(p["xn_fm"][:], p["pT"][:]), "ve")
            war.set("pT", "ve", xnr)
            if mode == "q" and not SKIP_ROUTING:
                emit_routing_tile(b, g, p, pers, (t0 + tt * 128 - a) // 128, war)
            for mi in range(0 if (mode == "q" and SKIP_QPROJ) else nmats):
                wsb = p["wB"] if mi == 1 else p["wA"]
                b.wait(pe, "ve", xnr)
                war.guard(pe, "pA")
                for mc in range(16):
                    if mode == "q":
                        xsrc = p["xnst"][:, mc * CH + tt * 128: mc * CH + (tt + 1) * 128]
                    else:
                        xsrc = p["xn_fm"][:, mc * 128:(mc + 1) * 128]
                    for j in range(4):
                        ins = pe.matmul(p["pA"][:, j * 512:(j + 1) * 512],
                                        xsrc,
                                        wsb[:, mc * 2048 + j * 512: mc * 2048 + (j + 1) * 512],
                                        start=(mc == 0), stop=(mc == 15))
                mmt = b.bump(ins, "pe")
                if mi == 0:
                    b.wait(ve, "pe", mmt)
                    b.wait(ve, "dsy", rc)
                    war.guard(ve, "ktm")
                    war.guard(ve, "sq")
                    pa3 = p["pA"].rearrange("p (h z) -> p h z", z=128)
                    ke, ko = pa3[:, :, 0:64], pa3[:, :, 64:128]
                    kt3 = p["ktm"].rearrange("p (h z) -> p h z", z=128)
                    cc = (p["rc"].rearrange("p (tb j) -> p tb j", j=64)[:, tt:tt + 1, :]
                          .broadcast_to([128, NH, 64]))
                    ss = (p["rs"].rearrange("p (tb j) -> p tb j", j=64)[:, tt:tt + 1, :]
                          .broadcast_to([128, NH, 64]))
                    sA = p["sq"][:, 0:1024].rearrange("p (h z) -> p h z", z=64)
                    sB = p["sq"][:, 1024:2048].rearrange("p (h z) -> p h z", z=64)
                    ve.tensor_tensor(sA[:], ke, cc, ALU.mult)
                    b.chain(ve.tensor_tensor(sB[:], ko, ss, ALU.mult), "ve")
                    ve.tensor_sub(kt3[:, :, 0:64], sA[:], sB[:])
                    ve.tensor_tensor(sA[:], ke, ss, ALU.mult)
                    rpd = b.chain(ve.tensor_tensor(sB[:], ko, cc, ALU.mult), "ve")
                    war.set("pA", "ve", rpd)
                    srd = b.chain(ve.tensor_add(kt3[:, :, 64:128], sA[:], sB[:]), "ve")
                    war.set("sq", "ve", srd)
                    b.wait(pe, "ve", srd)
                    war.guard(pe, "pT")
                    for hc in range(16):
                        ins = pe.transpose(p["pT"][:, hc * 128:(hc + 1) * 128],
                                           p["ktm"][:, hc * 128:(hc + 1) * 128], pers["idn"][:])
                    trt = b.bump(ins, "pe")
                    war.set("ktm", "pe", trt)
                    b.wait(ve, "pe", trt)
                    stname = "Kst" if mode == "kv" else "Qst"
                    war.guard(ve, stname)
                    sve = b.bump(ve.tensor_copy(
                        p[stname].rearrange("p (h q) -> p h q", q=CH)[:, :, tt * 128:(tt + 1) * 128],
                        p["pT"].rearrange("p (h q) -> p h q", q=128)[:, :, :]), "ve")
                    war.set("pT", "ve", sve)
                else:
                    b.wait(sc, "pe", mmt)
                    war.guard(sc, "vtm")
                    vc = b.bump(sc.activation(p["vtm"][:], p["pA"][:], AF.Copy), "sc")
                    war.set("pA", "sc", vc)
                    b.wait(sy, "sc", vc)
                    b.dma(sy, g["V_hbm"][:, t0 + tt * 128: t0 + (tt + 1) * 128, :]
                          .rearrange("h p d -> p h d"),
                          p["vtm"].rearrange("p (h d) -> p h d", d=128))
                    war.set("vtm", "dsy")
        std = b.tok("ve")
        b.wait(sy, "ve", std)
        if mode == "kv":
            b.dma(sy, g["K_hbm"][:, :, t0:t0 + w].rearrange("h d q -> d h q"),
                  p["Kst"].rearrange("p (h q) -> p h q", q=CH)[:, :, 0:w])
            war.set("Kst", "dsy")
        else:
            b.dma(sy, g["Q_hbm"][:, :, t0 - a:t0 - a + w].rearrange("h d q -> d h q"),
                  p["Qst"].rearrange("p (h q) -> p h q", q=CH)[:, :, 0:w])
            war.set("Qst", "dsy")
            b.dma(sy, g["xnown"][:, t0 - a:t0 - a + w].rearrange("(mc p) q -> p mc q", p=128),
                  p["xnst"].rearrange("p (mc q) -> p mc q", mc=16)[:, :, 0:w])
            war.set("xnst", "dsy")
        war.set("rc", "ve")
        war.set("rs", "ve")


def emit_routing_tile(b, g, p, pers, gidx, war):
    pe, ve, sc = b.pe, b.ve, b.sc
    xfd0 = b.tok("ve")
    for half in range(2):
        b.wait(pe, "ve", xfd0)
        war.guard(pe, "pA")
        for mc in range(8):
            c = half * 8 + mc
            ins = pe.transpose(p["pA"].bitcast(F32)[:, mc * 128:(mc + 1) * 128],
                               p["xn_f"][:, c * 128:(c + 1) * 128], pers["idnf"][:])
        tr = b.bump(ins, "pe")
        b.wait(ve, "pe", tr)
        war.guard(ve, "xnf_fm")
        b.chain(ve.tensor_copy(p["xnf_fm"][:, half * 1024:(half + 1) * 1024],
                               p["pA"][:, 0:1024]), "ve")
        war.set("pA", "ve")
    war.set("xn_f", "pe")
    xfd = b.tok("ve")
    b.wait(pe, "ve", xfd)
    war.guard(pe, "psg")
    for mc in range(16):
        ins = pe.matmul(p["psg"][:], p["xnf_fm"][:, mc * 128:(mc + 1) * 128],
                        pers["gateT"][:, mc * NE:(mc + 1) * NE],
                        start=(mc == 0), stop=(mc == 15))
    gmt = b.bump(ins, "pe")
    war.set("xnf_fm", "pe", gmt)
    b.wait(sc, "pe", gmt)
    sgd = b.bump(sc.activation(p["scores"][:], p["psg"][:], AF.Sigmoid), "sc")
    war.set("psg", "sc", sgd)
    sgd = b.fence("sc")
    b.wait(ve, "sc", sgd)
    b.chain(ve.tensor_add(p["vv"][:], p["scores"][:], pers["biasrep"][:]), "ve")
    b.chain(ve.reduce_max(p["r1"][:], p["vv"][:], axis=AX.X), "ve")
    b.chain(ve.tensor_scalar(p["m1m"][:], p["vv"][:], p["r1"][:], None, ALU.is_ge), "ve")
    b.chain(ve.scalar_tensor_tensor(p["v2"][:], p["m1m"][:], -1e30, p["vv"][:],
                                    ALU.mult, ALU.add), "ve")
    b.chain(ve.reduce_max(p["r2"][:], p["v2"][:], axis=AX.X), "ve")
    b.chain(ve.tensor_scalar(p["m2m"][:], p["v2"][:], p["r2"][:], None, ALU.is_ge), "ve")
    b.chain(ve.tensor_add(p["maskt"][:], p["m1m"][:], p["m2m"][:]), "ve")
    b.bump(ve.tensor_tensor(pers["s_mine"][:, gidx * NE:(gidx + 1) * NE],
                            p["scores"][:], p["maskt"][:], ALU.mult), "ve")


def emit_attention(b, g, p, pers, a, bb):
    pe, ve, sc, sy = b.pe, b.ve, b.sc, b.sy
    ntok = bb - a
    nkt = bb // 128
    SEG = 16
    war = War(b)

    for h in range(NH):
        hp = h % 2
        war.guard(sy, ("K", hp))
        b.dma(sy, p["K"][hp][:, 0:bb], g["K_hbm"][h, :, 0:bb])
        lv = b.dma(sy, p["V"][hp].rearrange("p (tb e) -> p tb e", e=129)[:, 0:nkt, 0:128],
                   g["V_hbm"][h, 0:bb, :].rearrange("(tb p) d -> p tb d", p=128))
        b.wait(pe, "dsy", lv)
        ohd = 0
        prev_p2ve = 0
        for qs, qw in _chunks(0, ntok, 512):
            qsg = a + qs
            qp = (qs // 512) % 2
            war.guard(sy, ("Q", qp))
            b.wait(sy, "ve", prev_p2ve)
            lq = b.dma(sy, p["Q"][qp][:, 0:qw], g["Q_hbm"][h, :, qs:qs + qw])
            b.wait(pe, "dsy", lq)
            ntile = qw // 128
            nkc = (qsg + qw) // 128
            nseg = (nkc + SEG - 1) // SEG
            if DBG and h == 0 and qs == 0:
                b.dma(sy, g["KSB"][:, 0:bb], p["K"][hp][:, 0:bb], "dbgs")
                b.dma(sy, g["VSB"][:, 0:nkt * 129], p["V"][hp][:, 0:nkt * 129], "dbgs")
            for sg in range(nseg):
                sp = sg % 2
                kc0, kc1 = sg * SEG, min((sg + 1) * SEG, nkc)
                pw3 = p["Pw"][sp].rearrange("p (k q) -> p k q", q=512)
                for kc in range(kc0, kc1):
                    kg = kc * 128
                    off = max(0, kg - qsg)
                    if off >= qw:
                        continue
                    sw = qw - off
                    spp = kc % 2
                    war.guard(pe, ("S", spp))
                    smm = b.bump(pe.matmul(p["S"][spp][:, 0:sw], p["K"][hp][:, kg:kg + 128],
                                           p["Q"][qp][:, off:off + sw],
                                           start=True, stop=True), "pe")
                    if kg >= qsg:
                        b.wait(ve, "pe", smm)
                        vmm = b.bump(ve.tensor_add(p["S"][spp][:, 0:128],
                                                   p["S"][spp][:, 0:128], pers["tri"][:]), "ve")
                        b.wait(sc, "ve", vmm)
                    else:
                        b.wait(sc, "pe", smm)
                    war.guard(sc, ("Pps", spp))
                    ex = b.bump(sc.activation(p["Pps"][spp][:, 0:sw],
                                              p["S"][spp][:, 0:sw], AF.Exp), "sc")
                    war.set(("S", spp), "sc", ex)
                    b.wait(ve, "sc", ex)
                    war.guard(ve, ("Pw", sp))
                    exv = b.chain(ve.tensor_copy(pw3[:, kc - kc0, 0:sw],
                                                 p["Pps"][spp][:, 0:sw]), "ve")
                    war.set(("Pps", spp), "ve", exv)
                exd = b.tok("ve")
                if DBG and h == 1 and qs == 0 and sg == 0:
                    b.wait(sy, "ve", exd)
                    b.dma(sy, g["PWD"], p["Pw"][sp][:], "dbgs")
                b.wait(pe, "ve", exd)
                for tt in range(ntile):
                    toff = tt * 128
                    last_kc = (qsg + toff) // 128
                    if last_kc < kc0:
                        continue
                    war.guard(pe, ("opsum", tt % 2))
                    ins = None
                    for kc in range(kc0, min(last_kc, kc1 - 1) + 1):
                        off = max(0, kc * 128 - qsg)
                        pcol = toff - off
                        ins = pe.matmul(p["oaug"][tt % 2][:, 0:129],
                                        pw3[:, kc - kc0, pcol:pcol + 128],
                                        p["V"][hp][:, kc * 129:kc * 129 + 129],
                                        start=(kc == kc0), stop=(kc == min(last_kc, kc1 - 1)))
                    pvt = b.bump(ins, "pe")
                    b.wait(ve, "pe", pvt)
                    oa = p["oacc"][:, tt * 129:(tt + 1) * 129]
                    if sg == 0:
                        war.guard(ve, ("oacc", tt))
                        cpd = b.chain(ve.tensor_copy(oa, p["oaug"][tt % 2][:]), "ve")
                    else:
                        cpd = b.chain(ve.tensor_tensor(oa, oa, p["oaug"][tt % 2][:], ALU.add), "ve")
                    war.set(("opsum", tt % 2), "ve", cpd)
                war.set(("Pw", sp), "pe")
            prev_p2ve = b.tok("ve")
            if DBG and h == 1 and qs == 0:
                b.wait(sy, "ve", b.tok("ve"))
                b.dma(sy, g["OACCD"], p["oacc"][:], "dbgs")
            for tt in range(ntile):
                oa = p["oacc"][:, tt * 129:(tt + 1) * 129]
                rsd = b.chain(ve.reciprocal(p["rs"][:], oa[:, 128:129]), "ve")
                war.guard(ve, "obf")
                ob = b.chain(ve.tensor_scalar(p["obf"][:], oa[:, 0:128], p["rs"][:], None,
                                              ALU.mult), "ve")
                war.set(("oacc", tt), "ve", ob)
                b.wait(pe, "ve", ob)
                war.guard(pe, "otp")
                otr = b.bump(pe.transpose(p["otp"][:, tt * 128:(tt + 1) * 128],
                                          p["obf"][:], pers["idn"][:]), "pe")
                war.set("obf", "pe", otr)
            trd = b.tok("pe")
            b.wait(ve, "pe", trd)
            war.guard(ve, "ohead")
            ohd = b.bump(ve.tensor_copy(p["ohead"][:, qs:qs + qw], p["otp"][:, 0:qw]), "ve")
            war.set("otp", "ve", ohd)
        b.wait(sy, "ve", ohd)
        if DBG and h == 1:
            b.dma(sy, g["OHD"][:, 0:ntok], p["ohead"][:, 0:ntok], "dbgs")
        b.dma(sy, g["o_hbm"][h, :, 0:ntok], p["ohead"][:, 0:ntok])
        war.set("ohead", "dsy")
        war.set(("K", hp), "pe")
        for qp_ in range(2):
            war.set(("Q", qp_), "pe")



def emit_ffn(b, g, p, pers, a, bb):
    pe, ve, sc, sy, gp = b.pe, b.ve, b.sc, b.sy, b.gp
    war = War(b)

    for s0, swid in _chunks(a, bb, SUB):
        nt = swid // 128
        soff = s0 - a
        war.guard(gp, "wo")
        lwo = b.dma(gp, p["wo"].rearrange("p (mc n) -> p mc n", mc=16),
                    g["wo2"].rearrange("(mc p) n -> p mc n", p=128), "dgp")
        war.guard(sy, "xns")
        lxn = b.dma(sy, p["xns"].rearrange("p (mc q) -> p mc q", mc=16)[:, :, 0:swid],
                    g["xnown"][:, soff:soff + swid].rearrange("(mc p) q -> p mc q", p=128))
        b.wait(pe, "dgp", lwo)
        b.wait(pe, "dsy", lxn)
        for tt in range(nt):
            otp = tt % 2
            war.guard(sy, ("ot", otp))
            lo = b.dma(sy, p["ot"][otp].rearrange("p (h q) -> p h q", h=NH),
                       g["o_hbm"][:, :, soff + tt * 128: soff + (tt + 1) * 128]
                       .rearrange("h d q -> d h q"))
            b.wait(pe, "dsy", lo)
            war.guard(pe, "ps2")
            for hc in range(NH):
                for j in range(4):
                    ins = pe.matmul(p["ps2"][:, j * 512:(j + 1) * 512],
                                    p["ot"][otp][:, hc * 128:(hc + 1) * 128],
                                    p["wo"][:, hc * 2048 + j * 512: hc * 2048 + (j + 1) * 512],
                                    start=(hc == 0), stop=(hc == NH - 1))
            opd = b.bump(ins, "pe")
            war.set(("ot", otp), "pe", opd)
            b.wait(ve, "pe", opd)
            b.bump(ve.tensor_copy(p["acc"][:, tt * 2048:(tt + 1) * 2048], p["ps2"][:]), "ve")
            war.set("ps2", "ve")
        war.set("wo", "pe")
        war.set(("w13", 0), "pe")
        war.set(("w13", 1), "pe")
        for e in range(NE + 1):
            shared = (e == NE)
            for hh in range(2):
                wp = (e * 2 + hh) % 2
                war.guard(gp, ("w13", wp))
                if shared:
                    s1ap = g["sw1T"][:, hh * 512:(hh + 1) * 512]
                    s3ap = g["sw3T"][:, hh * 512:(hh + 1) * 512]
                    s2ap = g["sw2T"][hh * 512:(hh + 1) * 512, :]
                else:
                    s1ap = g["w1T"][e, :, hh * 512:(hh + 1) * 512]
                    s3ap = g["w3T"][e, :, hh * 512:(hh + 1) * 512]
                    s2ap = g["w2T"][e, hh * 512:(hh + 1) * 512, :]
                b.dma(gp, p["w13"][wp].rearrange("p (mc z) -> p mc z", z=1024)[:, :, 0:512],
                      s1ap.rearrange("(mc p) z -> p mc z", p=128), "dgp")
                b.dma(gp, p["w13"][wp].rearrange("p (mc z) -> p mc z", z=1024)[:, :, 512:1024],
                      s3ap.rearrange("(mc p) z -> p mc z", p=128), "dgp")
                lw = b.dma(gp, p["w2"][wp].rearrange("p (hc m) -> p hc m", hc=4),
                           s2ap.rearrange("(hc p) m -> p hc m", p=128), "dgp")
                b.wait(pe, "dgp", lw)
                for tt in range(nt):
                    gidx = soff // 128 + tt
                    scl = None if shared else pers["s_mine"][:, gidx * NE + e: gidx * NE + e + 1]
                    war.guard(pe, "ps1")
                    for mc in range(16):
                        pe.matmul(p["ps1"][:, 0:512],
                                  p["xns"][:, mc * SUB + tt * 128: mc * SUB + (tt + 1) * 128],
                                  p["w13"][wp][:, mc * 1024: mc * 1024 + 512],
                                  start=(mc == 0), stop=(mc == 15))
                        ins = pe.matmul(p["ps1"][:, 512:1024],
                                        p["xns"][:, mc * SUB + tt * 128: mc * SUB + (tt + 1) * 128],
                                        p["w13"][wp][:, mc * 1024 + 512: mc * 1024 + 1024],
                                        start=(mc == 0), stop=(mc == 15))
                    m1d = b.bump(ins, "pe")
                    b.wait(sc, "pe", m1d)
                    war.guard(sc, "tsb")
                    if shared:
                        sil = b.bump(sc.activation(p["tsb"][:], p["ps1"][:, 0:512], AF.Silu), "sc")
                    else:
                        sil = b.bump(sc.activation(p["tsb"][:], p["ps1"][:, 0:512], AF.Silu,
                                                   scale=scl), "sc")
                    sil = b.fence("sc")
                    b.wait(ve, "sc", sil)
                    war.guard(ve, "hsb")
                    if shared:
                        hbd = b.chain(ve.tensor_tensor(p["hsb"][:], p["tsb"][:],
                                                       p["ps1"][:, 512:1024], ALU.mult), "ve")
                    else:
                        hbd = b.chain(ve.scalar_tensor_tensor(
                            p["hsb"][:], p["tsb"][:], scl,
                            p["ps1"][:, 512:1024], ALU.mult, ALU.mult), "ve")
                    war.set("ps1", "ve", hbd)
                    war.set("tsb", "ve", hbd)
                    b.wait(pe, "ve", hbd)
                    war.guard(pe, "htp")
                    for hc in range(4):
                        ins = pe.transpose(p["htp"][:, hc * 128:(hc + 1) * 128],
                                           p["hsb"][:, hc * 128:(hc + 1) * 128], pers["idn"][:])
                    htt = b.bump(ins, "pe")
                    war.set("hsb", "pe", htt)
                    b.wait(ve, "pe", htt)
                    war.guard(ve, "hT")
                    hte = b.chain(ve.tensor_copy(p["hT"][:], p["htp"][:]), "ve")
                    war.set("htp", "ve", hte)
                    b.wait(pe, "ve", hte)
                    war.guard(pe, "ps2")
                    for hc in range(4):
                        for j in range(4):
                            ins = pe.matmul(p["ps2"][:, j * 512:(j + 1) * 512],
                                            p["hT"][:, hc * 128:(hc + 1) * 128],
                                            p["w2"][wp][:, hc * 2048 + j * 512: hc * 2048 + (j + 1) * 512],
                                            start=(hc == 0), stop=(hc == 3))
                    m2d = b.bump(ins, "pe")
                    war.set("hT", "pe", m2d)
                    b.wait(sc, "pe", m2d)
                    war.guard(sc, "scr")
                    scd = b.bump(sc.activation(p["scr"][:], p["ps2"][:], AF.Copy), "sc")
                    war.set("ps2", "sc", scd)
                    scd = b.fence("sc")
                    b.wait(gp, "sc", scd)
                    gad = b.chain(gp.tensor_tensor(p["acc"][:, tt * 2048:(tt + 1) * 2048],
                                                   p["acc"][:, tt * 2048:(tt + 1) * 2048],
                                                   p["scr"][:], ALU.add), "gp")
                    war.set("scr", "gp", gad)
                war.set(("w13", wp), "pe")
        accd = b.tok("gp")
        for tt in range(nt):
            xp = tt % 2
            war.guard(sy, ("xr", xp))
            lx = b.dma(sy, p["xr"][xp][:], g["x2d"][s0 + tt * 128: s0 + (tt + 1) * 128, :])
            b.wait(ve, "dsy", lx)
            b.wait(ve, "gp", accd)
            war.guard(ve, ("xo", xp))
            od = b.bump(ve.tensor_add(p["xo"][xp][:], p["acc"][:, tt * 2048:(tt + 1) * 2048],
                                      p["xr"][xp][:]), "ve")
            war.set(("xr", xp), "ve", od)
            b.wait(sy, "ve", od)
            b.dma(sy, g["OUT"][soff + tt * 128: soff + (tt + 1) * 128, :], p["xo"][xp][:])
            war.set(("xo", xp), "dsy")
        war.set("acc", "ve")
        war.set("xns", "pe")
    b.wait(sy, "dsy", b.tok("dsy"))


def emit_phase(b, name, alloc_fn, common_fn, body_fn):
    nc = b.nc
    with ExitStack() as pctx:
        p = alloc_fn(pctx)
        if b.nphase > 0:
            with nc.Block(f"{name}_bar") as blk:
                phv = 5 * b.nphase
                for attr in ("tensor", "vector", "scalar", "gpsimd", "sync"):
                    def _mk(eng_attr):
                        def _f(eng):
                            eng.wait_ge(b.sem["phase"], phv)
                        return _f
                    getattr(blk, attr)(_mk(attr))
        if common_fn is not None:
            common_fn(p)
        if MODE_M1:
            b.rank = None
            body_fn(p, 0)
            b.wait(b.sy, "dsy", b.tok("dsy"))
            b.wait(b.gp, "dgp", b.tok("dgp"))
            for eng in b.engs:
                b.bump(eng.nop(), "phase")
            b.nphase += 1
            return
        for arm in nc.switch_core_id(b.engs, n=128):
            r = arm.logical % 8
            for eng in b.engs:
                eng.br(f"{name}_body{r}")
        for r in range(NCORE):
            with nc.body(f"{name}_body{r}", valid_engines=ALL_EN):
                b.rank = r
                body_fn(p, r)
                b.wait(b.sy, "dsy", b.tok("dsy"))
                b.wait(b.gp, "dgp", b.tok("dgp"))
                for eng in b.engs:
                    b.bump(eng.nop(), "phase")
                for eng in b.engs:
                    eng.br(f"{name}_join")
            b.rank = None
        nc.switch_bb(f"{name}_join")
        b.nphase += 1


def build_nc(nmax, bounds):
    nc = bass.Bass("TRN2", target_bir_lowering=False, debug=False, num_devices=NCORE)
    g = declare_io(nc, nmax)
    ctx = ExitStack()
    with ctx:
        b = BState(nc, ctx)
        pers = {}
        maxg = nmax // 128
        pers["idn"] = ctx.enter_context(nc.sbuf_tensor("idn_s", [128, 128], BF16))
        pers["idnf"] = ctx.enter_context(nc.sbuf_tensor("idnf_s", [128, 128], F32))
        pers["tri"] = ctx.enter_context(nc.sbuf_tensor("tri_s", [128, 128], F32))
        pers["gateT"] = ctx.enter_context(nc.sbuf_tensor("gateTs", [128, 16 * NE], F32))
        pers["biasrep"] = ctx.enter_context(nc.sbuf_tensor("biasreps", [128, NE], F32))
        pers["s_mine"] = ctx.enter_context(nc.sbuf_tensor("s_mine", [128, maxg * NE], F32))
        pers["epst"] = ctx.enter_context(nc.sbuf_tensor("epst", [128, 1], F32))
        pers["acttmp"] = ctx.enter_context(nc.sbuf_tensor("acttmp", [128, 1], F32))

        with nc.Block("prol") as blk:
            @blk.sync
            def _(sy):
                b.dma(sy, pers["idn"][:], g["idn"])
                b.dma(sy, pers["idnf"][:], g["idnf"])
                b.dma(sy, pers["tri"][:], g["tri"])
                b.dma(sy, pers["gateT"].rearrange("p (mc e) -> p mc e", e=NE),
                      g["gateT"].rearrange("(mc p) e -> p mc e", p=128))
                b.dma(sy, pers["biasrep"][:], g["biasrep"])
            dtok = b.tok("dsy")
            @blk.tensor
            def _(te):
                b.wait(te, "dsy", dtok)
            @blk.vector
            def _(ve):
                b.wait(ve, "dsy", dtok)
                b.bump(ve.memset(pers["epst"][:], 1e-5), "ve")
            vtok0 = b.tok("ve")
            @blk.scalar
            def _(sc):
                b.wait(sc, "dsy", dtok)
                b.wait(sc, "ve", vtok0)
            @blk.gpsimd
            def _(gp_):
                b.wait(gp_, "dsy", dtok)

        def alloc_p1(pctx):
            e = pctx.enter_context
            p = {}
            p["wA"] = e(nc.sbuf_tensor("wk_sb", [128, 16 * 2048], BF16))
            p["wB"] = e(nc.sbuf_tensor("wv_sb", [128, 16 * 2048], BF16))
            p["x"] = e(nc.sbuf_tensor("x_sb", [128, 2048], F32))
            p["rc"] = e(nc.sbuf_tensor("rc_sb", [128, 4 * 64], F32))
            p["rs"] = e(nc.sbuf_tensor("rs_sb", [128, 4 * 64], F32))
            p["xn_fm"] = e(nc.sbuf_tensor("xn_fm", [128, 2048], BF16))
            p["xn_bf"] = e(nc.sbuf_tensor("xn_bf", [128, 2048], BF16))
            p["ktm"] = e(nc.sbuf_tensor("ktm", [128, 2048], BF16))
            p["vtm"] = e(nc.sbuf_tensor("vtm", [128, 2048], BF16))
            p["Kst"] = e(nc.sbuf_tensor("Kst", [128, NH * CH], BF16))
            p["sq"] = e(nc.sbuf_tensor("sq", [128, 2048], F32))
            for nm in ("msum", "sqs", "mean", "sqm", "m2", "var", "sd", "rstd"):
                p[nm] = e(nc.sbuf_tensor(nm, [128, 1], F32))
            p["pA"] = e(nc.psum_tensor("pA", [128, 2048], F32))
            p["pT"] = e(nc.psum_tensor("pT", [128, 2048], BF16))
            return p

        def common_p1(p):
            with nc.Block("ldkv") as blk:
                @blk.scalar
                def _(sc_):
                    sc_.activation(pers["acttmp"][:], pers["epst"][:], AF.Sqrt)
                @blk.gpsimd
                def _(gp_):
                    b.dma(gp_, p["wA"].rearrange("p (mc n) -> p mc n", mc=16),
                          g["wk2"].rearrange("(mc p) n -> p mc n", p=128), "dgp")
                    b.dma(gp_, p["wB"].rearrange("p (mc n) -> p mc n", mc=16),
                          g["wv2"].rearrange("(mc p) n -> p mc n", p=128), "dgp")
                dtok = b.tok("dgp")
                @blk.tensor
                def _(te):
                    b.wait(te, "dgp", dtok)

        def body_p1(p, r):
            if ONLY_PHASES is None or "p1" in ONLY_PHASES:
                emit_ctx_pass(b, g, p, pers, 0, bounds[r], "kv")

        emit_phase(b, "p1", alloc_p1, common_p1, body_p1)

        def alloc_p15(pctx):
            e = pctx.enter_context
            p = {}
            p["wA"] = e(nc.sbuf_tensor("b_wq_sb", [128, 16 * 2048], BF16))
            p["x"] = e(nc.sbuf_tensor("b_x_sb", [128, 2048], F32))
            p["rc"] = e(nc.sbuf_tensor("b_rc_sb", [128, 4 * 64], F32))
            p["rs"] = e(nc.sbuf_tensor("b_rs_sb", [128, 4 * 64], F32))
            p["xnst"] = e(nc.sbuf_tensor("b_xnst", [128, 16 * CH], BF16))
            p["xn_bf"] = e(nc.sbuf_tensor("b_xn_bf", [128, 2048], BF16))
            p["xn_f"] = e(nc.sbuf_tensor("b_xn_f", [128, 2048], F32))
            p["ktm"] = e(nc.sbuf_tensor("b_ktm", [128, 2048], BF16))
            p["Qst"] = e(nc.sbuf_tensor("b_Qst", [128, NH * CH], BF16))
            p["sq"] = e(nc.sbuf_tensor("b_sq", [128, 2048], F32))
            p["xnf_fm"] = e(nc.sbuf_tensor("b_xnf_fm", [128, 2048], F32))
            for nm in ("msum", "sqs", "mean", "sqm", "m2", "var", "sd", "rstd"):
                p[nm] = e(nc.sbuf_tensor("b_" + nm, [128, 1], F32))
            for nm in ("scores", "vv", "m1m", "v2", "m2m", "maskt"):
                p[nm] = e(nc.sbuf_tensor("b_" + nm, [128, NE], F32))
            for nm in ("r1", "r2"):
                p[nm] = e(nc.sbuf_tensor("b_" + nm, [128, 1], F32))
            p["pA"] = e(nc.psum_tensor("b_pA", [128, 2048], F32))
            p["pT"] = e(nc.psum_tensor("b_pT", [128, 2048], BF16))
            p["psg"] = e(nc.psum_tensor("b_psg", [128, NE], F32))
            return p

        def common_p15(p):
            with nc.Block("ldq") as blk:
                phv = 5 * b.nphase
                @blk.scalar
                def _(sc_):
                    sc_.wait_ge(b.sem["phase"], phv)
                    sc_.activation(pers["acttmp"][:], pers["epst"][:], AF.Sigmoid)
                @blk.gpsimd
                def _(gp_):
                    gp_.wait_ge(b.sem["phase"], phv)
                    b.dma(gp_, p["wA"].rearrange("p (mc n) -> p mc n", mc=16),
                          g["wq2"].rearrange("(mc p) n -> p mc n", p=128), "dgp")
                dtok = b.tok("dgp")
                @blk.tensor
                def _(te):
                    b.wait(te, "dgp", dtok)

        def body_p15(p, r):
            if ONLY_PHASES is None or "p15" in ONLY_PHASES:
                a = 0 if r == 0 else bounds[r - 1]
                emit_ctx_pass(b, g, p, pers, a, bounds[r], "q", a=a)

        emit_phase(b, "p15", alloc_p15, common_p15, body_p15)

        def alloc_p2(pctx):
            e = pctx.enter_context
            p = {}
            nkt_max = bounds[-1] // 128
            p["K"] = [e(nc.sbuf_tensor(f"c_K{i}", [128, bounds[-1]], BF16)) for i in range(2)]
            p["V"] = [e(nc.sbuf_tensor(f"c_V{i}", [128, nkt_max * 129], BF16)) for i in range(2)]
            p["Q"] = [e(nc.sbuf_tensor(f"c_Q{i}", [128, 512], BF16)) for i in range(2)]
            p["Pw"] = [e(nc.sbuf_tensor(f"c_Pw{i}", [128, 16 * 512], BF16)) for i in range(2)]
            p["oacc"] = e(nc.sbuf_tensor("c_oacc", [128, 4 * 129], F32))
            p["ohead"] = e(nc.sbuf_tensor("c_ohead", [128, nmax], BF16))
            p["obf"] = e(nc.sbuf_tensor("c_obf", [128, 128], BF16))
            p["rs"] = e(nc.sbuf_tensor("c_rs", [128, 1], F32))
            p["S"] = [e(nc.psum_tensor(f"c_S{i}", [128, 512], F32)) for i in range(2)]
            p["Pps"] = [e(nc.psum_tensor(f"c_Pps{i}", [128, 512], F32)) for i in range(2)]
            p["oaug"] = [e(nc.psum_tensor(f"c_oaug{i}", [128, 129], F32)) for i in range(2)]
            p["otp"] = e(nc.psum_tensor("c_otp", [128, 512], BF16))
            return p

        def common_p2(p):
            with nc.Block("p2init") as blk:
                phv = 5 * b.nphase
                @blk.scalar
                def _(sc_):
                    sc_.wait_ge(b.sem["phase"], phv)
                    sc_.activation(pers["acttmp"][:], pers["epst"][:], AF.Exp)
                @blk.vector
                def _(ve):
                    ve.wait_ge(b.sem["phase"], phv)
                    for i in range(2):
                        b.bump(ve.memset(
                            p["V"][i].rearrange("p (tb e) -> p tb e", e=129)[:, :, 128:129],
                            1.0), "ve")
                vtok = b.tok("ve")
                @blk.sync
                def _(sy):
                    b.wait(sy, "ve", vtok)

        def body_p2(p, r):
            if ONLY_PHASES is None or "p2" in ONLY_PHASES:
                a = 0 if r == 0 else bounds[r - 1]
                emit_attention(b, g, p, pers, a, bounds[r])

        emit_phase(b, "p2", alloc_p2, common_p2, body_p2)

        def alloc_p3(pctx):
            e = pctx.enter_context
            p = {}
            arena = e(nc.sbuf_tensor("d_war3", [128, 49152], BF16))
            p["wo"] = arena[:, 0:32768]
            p["w13"] = [arena[:, 0:16384], arena[:, 16384:32768]]
            p["w2"] = [arena[:, 32768:40960], arena[:, 40960:49152]]
            p["ot"] = [e(nc.sbuf_tensor(f"d_ot{i}", [128, NH * 128], BF16)) for i in range(2)]
            p["xns"] = e(nc.sbuf_tensor("d_xns", [128, 16 * SUB], BF16))
            p["acc"] = e(nc.sbuf_tensor("d_acc", [128, (SUB // 128) * 2048], F32))
            p["scr"] = e(nc.sbuf_tensor("d_scr", [128, 2048], F32))
            p["tsb"] = e(nc.sbuf_tensor("d_tsb", [128, 512], F32))
            p["hsb"] = e(nc.sbuf_tensor("d_hsb", [128, 512], BF16))
            p["hT"] = e(nc.sbuf_tensor("d_hT", [128, 512], BF16))
            p["xr"] = [e(nc.sbuf_tensor(f"d_xr{i}", [128, 2048], F32)) for i in range(2)]
            p["xo"] = [e(nc.sbuf_tensor(f"d_xo{i}", [128, 2048], F32)) for i in range(2)]
            p["ps1"] = e(nc.psum_tensor("d_ps1", [128, 1024], F32))
            p["ps2"] = e(nc.psum_tensor("d_ps2", [128, 2048], F32))
            p["htp"] = e(nc.psum_tensor("d_htp", [128, 512], BF16))
            return p

        def common_p3(p):
            with nc.Block("p3w") as blk:
                phv = 5 * b.nphase
                @blk.scalar
                def _(sc_):
                    sc_.wait_ge(b.sem["phase"], phv)
                    sc_.activation(pers["acttmp"][:], pers["epst"][:], AF.Silu)
                @blk.gpsimd
                def _(gp_):
                    gp_.wait_ge(b.sem["phase"], phv)

        def body_p3(p, r):
            if ONLY_PHASES is None or "p3" in ONLY_PHASES:
                a = 0 if r == 0 else bounds[r - 1]
                emit_ffn(b, g, p, pers, a, bounds[r])

        emit_phase(b, "p3", alloc_p3, common_p3, body_p3)
        if not MODE_M1:
            nc.switch_bb("final_bb")
        if DBG:
            with nc.Block("dbg") as blk:
                phv = 5 * b.nphase
                @blk.gpsimd
                def _(gp_):
                    gp_.wait_ge(b.sem["phase"], phv)
                    b.dma(gp_, g["KDBG"], g["K_hbm"], "dbgs")
                    b.dma(gp_, g["VDBG"], g["V_hbm"], "dbgs")
                    b.dma(gp_, g["QDBG"], g["Q_hbm"], "dbgs")
                    b.dma(gp_, g["ODBG"], g["o_hbm"], "dbgs")
                    b.dma(gp_, g["XNDBG"], g["xnown"], "dbgs")
                    b.dma(gp_, g["SDBG"][:, 0:pers["s_mine"].shape[1]], pers["s_mine"][:], "dbgs")
                    gp_.wait_ge(b.sem["dbgs"], b.tok("dbgs"))
    return nc


def kernel(**inputs):
    hi = host_inputs(inputs)
    nmax = T if MODE_M1 else NMAX
    bounds = [T] * NCORE if MODE_M1 else BOUNDS
    nc = build_nc(nmax, bounds)
    in_maps = [dict(hi) for _ in range(NCORE)]
    res = run_bass_kernel_spmd(nc, in_maps, core_ids=list(range(NCORE)))
    if MODE_M1:
        out = res.results[0]["OUT"]
    else:
        parts = []
        a = 0
        for r in range(NCORE):
            parts.append(res.results[r]["OUT"][0:bounds[r] - a, :])
            a = bounds[r]
        out = np.concatenate(parts, axis=0)
    return out.reshape(1, T, M).astype(np.float32)


# version 08 (final)
# speedup vs baseline: 1.0033x; 1.0033x over previous
import numpy as np
import ml_dtypes
from contextlib import ExitStack

import concourse.bass as bass
import concourse.mybir as mybir
from concourse.bass_utils import run_bass_kernel_spmd

F32 = mybir.dt.float32
BF16 = mybir.dt.bfloat16
AF = mybir.ActivationFunctionType
ALU = mybir.AluOpType
AX = mybir.AxisListType
EN = mybir.EngineType

T = 8192
M = 2048
NH = 16
D = 128
NE = 8
HID = 1024
NCORE = 8
CH = 512
SUB = 512
ATTN_MULT = float(np.float32((1.0 / 128.0) ** 0.5))

BOUNDS = [1664, 3072, 4224, 5248, 6144, 6912, 7552, 8192]
NMAX = 1664

ALL_EN = [EN.PE, EN.DVE, EN.Activation, EN.Pool, EN.SP]

MODE_M1 = False
DBG = False
SKIP_ROUTING = False
SKIP_QPROJ = False
ONLY_PHASES = None


def _chunks(t0, t1, step):
    out = []
    t = t0
    while t < t1:
        w = min(step, t1 - t)
        out.append((t, w))
        t += w
    return out


class BState:
    def __init__(self, nc, ctx):
        self.nc = nc
        self.pe, self.ve, self.sc = nc.tensor, nc.vector, nc.scalar
        self.gp, self.sy = nc.gpsimd, nc.sync
        self.engs = [self.pe, self.ve, self.sc, self.gp, self.sy]
        self.sem = {}
        for nm in ("pe", "ve", "sc", "gp", "sy", "dsy", "dgp", "phase", "dbgs"):
            self.sem[nm] = ctx.enter_context(nc.semaphore("sem_" + nm))
        self.cnt = {r: {nm: 0 for nm in self.sem} for r in range(NCORE)}
        self.rank = None
        self.nphase = 0

    def _ranks(self):
        return range(NCORE) if self.rank is None else (self.rank,)

    def tok(self, nm):
        return self.cnt[0 if self.rank is None else self.rank][nm]

    def bump(self, instr, nm, n=None):
        if n is None:
            n = 16 if nm.startswith("d") else 1
        instr.then_inc(self.sem[nm], n)
        for r in self._ranks():
            self.cnt[r][nm] += n
        return self.tok(nm)

    def dma(self, eng, out, in_, q="dsy", **kw):
        ins = eng.dma_start(out=out, in_=in_, **kw)
        return self.bump(ins, q, 16)

    def wait(self, eng, nm, val):
        if val > 0:
            eng.wait_ge(self.sem[nm], val)

    def chain(self, instr, nm):
        tok = self.bump(instr, nm)
        self.eng_of(nm).wait_ge(self.sem[nm], tok)
        return tok

    def eng_of(self, nm):
        return {"pe": self.pe, "ve": self.ve, "sc": self.sc,
                "gp": self.gp, "sy": self.sy}[nm]

    def fence(self, nm):
        eng = self.eng_of(nm)
        eng.wait_ge(self.sem[nm], self.tok(nm))
        return self.bump(eng.nop(), nm)


class War:

    def __init__(self, b):
        self.b = b
        self.d = {}

    def set(self, key, nm, val=None):
        self.d[key] = [(nm, self.b.tok(nm) if val is None else val)]

    def add(self, key, nm, val=None):
        self.d.setdefault(key, []).append((nm, self.b.tok(nm) if val is None else val))

    def guard(self, eng, key):
        for nm, val in self.d.get(key, ()):
            self.b.wait(eng, nm, val)


def host_inputs(inp):
    x = np.ascontiguousarray(np.asarray(inp["L0_attn_x"], dtype=np.float32).reshape(T, M))
    wq = np.ascontiguousarray(np.asarray(inp["L0_attn_wq"], np.float32).reshape(M, NH * D))
    wk = np.ascontiguousarray(np.asarray(inp["L0_attn_wk"], np.float32).reshape(M, NH * D))
    wv = np.ascontiguousarray(np.asarray(inp["L0_attn_wv"], np.float32).reshape(M, NH * D))
    wo = np.ascontiguousarray(np.asarray(inp["L0_attn_wo"], np.float32).reshape(NH * D, M))
    w1T = np.ascontiguousarray(np.asarray(inp["L0_ffn_w1"], np.float32).transpose(0, 2, 1))
    w3T = np.ascontiguousarray(np.asarray(inp["L0_ffn_w3"], np.float32).transpose(0, 2, 1))
    w2T = np.ascontiguousarray(np.asarray(inp["L0_ffn_w2"], np.float32).transpose(0, 2, 1))
    sw1T = np.ascontiguousarray(np.asarray(inp["L0_ffn_shared_w1"], np.float32).T)
    sw3T = np.ascontiguousarray(np.asarray(inp["L0_ffn_shared_w3"], np.float32).T)
    sw2T = np.ascontiguousarray(np.asarray(inp["L0_ffn_shared_w2"], np.float32).T)
    gateT = np.ascontiguousarray(np.asarray(inp["L0_ffn_gate"], np.float32).T)
    biasrep = np.ascontiguousarray(
        np.broadcast_to(np.asarray(inp["L0_ffn_expert_bias"], np.float32)[None, :], (128, NE)))
    pos = np.arange(T, dtype=np.float32)[:, None]
    freqs = np.power(np.float32(10000.0),
                     -np.arange(64, dtype=np.float32) / np.float32(64.0))[None, :]
    rad = (pos * freqs).astype(np.float32)
    ropec = (np.cos(rad) * np.float32(ATTN_MULT)).astype(np.float32)
    ropes = (np.sin(rad) * np.float32(ATTN_MULT)).astype(np.float32)
    tri = np.where(np.arange(128)[:, None] > np.arange(128)[None, :],
                   np.float32(-1e30), np.float32(0.0)).astype(np.float32)
    return {
        "x2d": x, "wq2": wq, "wk2": wk, "wv2": wv, "wo2": wo,
        "w1T": w1T, "w3T": w3T, "w2T": w2T,
        "sw1T": sw1T, "sw3T": sw3T, "sw2T": sw2T,
        "gateT": gateT, "biasrep": biasrep,
        "ropec": ropec, "ropes": ropes, "tri": tri,
        "idn": np.eye(128, dtype=ml_dtypes.bfloat16),
        "idnf": np.eye(128, dtype=np.float32),
    }


def declare_io(nc, nmax):
    g = {}
    g["x2d"] = nc.dram_tensor("x2d", [T, M], F32, kind="ExternalInput").ap()
    for nm in ("wq2", "wk2", "wv2", "wo2"):
        g[nm] = nc.dram_tensor(nm, [M, M], F32, kind="ExternalInput").ap()
    g["w1T"] = nc.dram_tensor("w1T", [NE, M, HID], F32, kind="ExternalInput").ap()
    g["w3T"] = nc.dram_tensor("w3T", [NE, M, HID], F32, kind="ExternalInput").ap()
    g["w2T"] = nc.dram_tensor("w2T", [NE, HID, M], F32, kind="ExternalInput").ap()
    g["sw1T"] = nc.dram_tensor("sw1T", [M, HID], F32, kind="ExternalInput").ap()
    g["sw3T"] = nc.dram_tensor("sw3T", [M, HID], F32, kind="ExternalInput").ap()
    g["sw2T"] = nc.dram_tensor("sw2T", [HID, M], F32, kind="ExternalInput").ap()
    g["gateT"] = nc.dram_tensor("gateT", [M, NE], F32, kind="ExternalInput").ap()
    g["biasrep"] = nc.dram_tensor("biasrep", [128, NE], F32, kind="ExternalInput").ap()
    g["ropec"] = nc.dram_tensor("ropec", [T, 64], F32, kind="ExternalInput").ap()
    g["ropes"] = nc.dram_tensor("ropes", [T, 64], F32, kind="ExternalInput").ap()
    g["tri"] = nc.dram_tensor("tri", [128, 128], F32, kind="ExternalInput").ap()
    g["idn"] = nc.dram_tensor("idn", [128, 128], BF16, kind="ExternalInput").ap()
    g["idnf"] = nc.dram_tensor("idnf", [128, 128], F32, kind="ExternalInput").ap()
    g["OUT"] = nc.dram_tensor("OUT", [nmax, M], F32, kind="ExternalOutput").ap()
    g["K_hbm"] = nc.dram_tensor("K_hbm", [NH, D, T], BF16, kind="Internal").ap()
    g["V_hbm"] = nc.dram_tensor("V_hbm", [NH, T, D], BF16, kind="Internal").ap()
    g["Q_hbm"] = nc.dram_tensor("Q_hbm", [NH, D, nmax], BF16, kind="Internal").ap()
    g["o_hbm"] = nc.dram_tensor("o_hbm", [NH, D, nmax], BF16, kind="Internal").ap()
    g["xnown"] = nc.dram_tensor("xnown", [M, nmax], BF16, kind="Internal").ap()
    if DBG:
        g["KDBG"] = nc.dram_tensor("KDBG", [NH, D, T], BF16, kind="ExternalOutput").ap()
        g["KSB"] = nc.dram_tensor("KSB", [128, T], BF16, kind="ExternalOutput").ap()
        g["VSB"] = nc.dram_tensor("VSB", [128, (T // 128) * 129], BF16, kind="ExternalOutput").ap()
        g["PWD"] = nc.dram_tensor("PWD", [128, 16 * 512], BF16, kind="ExternalOutput").ap()
        g["OACCD"] = nc.dram_tensor("OACCD", [128, 4 * 129], F32, kind="ExternalOutput").ap()
        g["OHD"] = nc.dram_tensor("OHD", [128, nmax], BF16, kind="ExternalOutput").ap()
        g["VDBG"] = nc.dram_tensor("VDBG", [NH, T, D], BF16, kind="ExternalOutput").ap()
        g["QDBG"] = nc.dram_tensor("QDBG", [NH, D, nmax], BF16, kind="ExternalOutput").ap()
        g["ODBG"] = nc.dram_tensor("ODBG", [NH, D, nmax], BF16, kind="ExternalOutput").ap()
        g["XNDBG"] = nc.dram_tensor("XNDBG", [M, nmax], BF16, kind="ExternalOutput").ap()
        g["SDBG"] = nc.dram_tensor("SDBG", [128, 128 * NE], F32, kind="ExternalOutput").ap()
    return g


def emit_ctx_pass(b, g, p, pers, t0g, t1g, mode, a=0):
    pe, ve, sc, sy = b.pe, b.ve, b.sc, b.sy
    war = War(b)
    nmats = 2 if mode == "kv" else 1

    for (t0, w) in _chunks(t0g, t1g, CH):
        tw = w // 128
        war.guard(sy, "rc")
        rc = b.dma(sy, p["rc"].rearrange("p (tb j) -> p tb j", j=64)[:, 0:tw, :],
                   g["ropec"][t0:t0 + w, :].rearrange("(tb p) j -> p tb j", p=128))
        war.guard(sy, "rs")
        rc = b.dma(sy, p["rs"].rearrange("p (tb j) -> p tb j", j=64)[:, 0:tw, :],
                   g["ropes"][t0:t0 + w, :].rearrange("(tb p) j -> p tb j", p=128))
        for tt in range(tw):
            war.guard(sy, "x")
            xl = b.dma(sy, p["x"][:], g["x2d"][t0 + tt * 128: t0 + (tt + 1) * 128, :])
            xt = p["x"]
            b.wait(ve, "dsy", xl)
            b.wait(sc, "dsy", xl)
            b.chain(ve.reduce_sum(p["msum"][:], xt[:], axis=AX.X), "ve")
            war.guard(sc, "sq")
            sqd = b.bump(sc.activation(p["sq"][:], xt[:], AF.Square), "sc")
            b.wait(ve, "sc", sqd)
            b.chain(ve.reduce_sum(p["sqs"][:], p["sq"][:], axis=AX.X), "ve")
            war.set("sq", "ve")
            b.chain(ve.tensor_scalar(p["mean"][:], p["msum"][:], 1.0 / M, None, ALU.mult), "ve")
            b.chain(ve.tensor_scalar(p["sqm"][:], p["sqs"][:], 1.0 / M, None, ALU.mult), "ve")
            b.chain(ve.tensor_tensor(p["m2"][:], p["mean"][:], p["mean"][:], ALU.mult), "ve")
            b.chain(ve.tensor_sub(p["var"][:], p["sqm"][:], p["m2"][:]), "ve")
            vd = b.chain(ve.tensor_scalar(p["var"][:], p["var"][:], pers["epst"][:], None,
                                          ALU.add), "ve")
            b.wait(sc, "ve", vd)
            sdd = b.bump(sc.activation(p["sd"][:], p["var"][:], AF.Sqrt), "sc")
            b.wait(ve, "sc", sdd)
            b.chain(ve.reciprocal(p["rstd"][:], p["sd"][:]), "ve")
            for _nr in range(2):
                b.chain(ve.tensor_tensor(p["m2"][:], p["rstd"][:], p["rstd"][:], ALU.mult), "ve")
                b.chain(ve.tensor_tensor(p["m2"][:], p["m2"][:], p["var"][:], ALU.mult), "ve")
                b.chain(ve.tensor_scalar(p["m2"][:], p["m2"][:], -0.5, 1.5, ALU.mult, ALU.add), "ve")
                b.chain(ve.tensor_tensor(p["rstd"][:], p["rstd"][:], p["m2"][:], ALU.mult), "ve")
            war.guard(ve, "xn_bf")
            xnb = b.chain(ve.tensor_scalar(p["xn_bf"][:], xt[:], p["mean"][:], p["rstd"][:],
                                           ALU.subtract, ALU.mult), "ve")
            if mode == "q":
                war.guard(ve, "xn_f")
                xnb = b.chain(ve.tensor_scalar(p["xn_f"][:], xt[:], p["mean"][:], p["rstd"][:],
                                               ALU.subtract, ALU.mult), "ve")
            war.set("x", "ve")
            b.wait(pe, "ve", xnb)
            war.guard(pe, "pT")
            for mc in range(16):
                ins = pe.transpose(p["pT"][:, mc * 128:(mc + 1) * 128],
                                   p["xn_bf"][:, mc * 128:(mc + 1) * 128], pers["idn"][:])
            xtt = b.bump(ins, "pe")
            war.set("xn_bf", "pe", xtt)
            b.wait(ve, "pe", xtt)
            if mode == "q":
                war.guard(ve, "xnst")
                xnr = b.bump(ve.tensor_copy(
                    p["xnst"].rearrange("p (mc q) -> p mc q", q=CH)[:, :, tt * 128:(tt + 1) * 128],
                    p["pT"].rearrange("p (mc q) -> p mc q", q=128)[:, :, :]), "ve")
            else:
                war.guard(ve, "xn_fm")
                xnr = b.bump(ve.tensor_copy(p["xn_fm"][:], p["pT"][:]), "ve")
            war.set("pT", "ve", xnr)
            if mode == "q" and not SKIP_ROUTING:
                emit_routing_tile(b, g, p, pers, (t0 + tt * 128 - a) // 128, war)
            for mi in range(0 if (mode == "q" and SKIP_QPROJ) else nmats):
                wsb = p["wB"] if mi == 1 else p["wA"]
                b.wait(pe, "ve", xnr)
                war.guard(pe, "pA")
                for mc in range(16):
                    if mode == "q":
                        xsrc = p["xnst"][:, mc * CH + tt * 128: mc * CH + (tt + 1) * 128]
                    else:
                        xsrc = p["xn_fm"][:, mc * 128:(mc + 1) * 128]
                    for j in range(4):
                        ins = pe.matmul(p["pA"][:, j * 512:(j + 1) * 512],
                                        xsrc,
                                        wsb[:, mc * 2048 + j * 512: mc * 2048 + (j + 1) * 512],
                                        start=(mc == 0), stop=(mc == 15))
                mmt = b.bump(ins, "pe")
                if mi == 0:
                    b.wait(ve, "pe", mmt)
                    b.wait(ve, "dsy", rc)
                    war.guard(ve, "ktm")
                    war.guard(ve, "sq")
                    pa3 = p["pA"].rearrange("p (h z) -> p h z", z=128)
                    ke, ko = pa3[:, :, 0:64], pa3[:, :, 64:128]
                    kt3 = p["ktm"].rearrange("p (h z) -> p h z", z=128)
                    cc = (p["rc"].rearrange("p (tb j) -> p tb j", j=64)[:, tt:tt + 1, :]
                          .broadcast_to([128, NH, 64]))
                    ss = (p["rs"].rearrange("p (tb j) -> p tb j", j=64)[:, tt:tt + 1, :]
                          .broadcast_to([128, NH, 64]))
                    sA = p["sq"][:, 0:1024].rearrange("p (h z) -> p h z", z=64)
                    sB = p["sq"][:, 1024:2048].rearrange("p (h z) -> p h z", z=64)
                    ve.tensor_tensor(sA[:], ke, cc, ALU.mult)
                    b.chain(ve.tensor_tensor(sB[:], ko, ss, ALU.mult), "ve")
                    ve.tensor_sub(kt3[:, :, 0:64], sA[:], sB[:])
                    ve.tensor_tensor(sA[:], ke, ss, ALU.mult)
                    rpd = b.chain(ve.tensor_tensor(sB[:], ko, cc, ALU.mult), "ve")
                    war.set("pA", "ve", rpd)
                    srd = b.chain(ve.tensor_add(kt3[:, :, 64:128], sA[:], sB[:]), "ve")
                    war.set("sq", "ve", srd)
                    b.wait(pe, "ve", srd)
                    war.guard(pe, "pT")
                    for hc in range(16):
                        ins = pe.transpose(p["pT"][:, hc * 128:(hc + 1) * 128],
                                           p["ktm"][:, hc * 128:(hc + 1) * 128], pers["idn"][:])
                    trt = b.bump(ins, "pe")
                    war.set("ktm", "pe", trt)
                    b.wait(ve, "pe", trt)
                    stname = "Kst" if mode == "kv" else "Qst"
                    war.guard(ve, stname)
                    sve = b.bump(ve.tensor_copy(
                        p[stname].rearrange("p (h q) -> p h q", q=CH)[:, :, tt * 128:(tt + 1) * 128],
                        p["pT"].rearrange("p (h q) -> p h q", q=128)[:, :, :]), "ve")
                    war.set("pT", "ve", sve)
                else:
                    b.wait(sc, "pe", mmt)
                    war.guard(sc, "vtm")
                    vc = b.bump(sc.activation(p["vtm"][:], p["pA"][:], AF.Copy), "sc")
                    war.set("pA", "sc", vc)
                    b.wait(sy, "sc", vc)
                    b.dma(sy, g["V_hbm"][:, t0 + tt * 128: t0 + (tt + 1) * 128, :]
                          .rearrange("h p d -> p h d"),
                          p["vtm"].rearrange("p (h d) -> p h d", d=128))
                    war.set("vtm", "dsy")
        std = b.tok("ve")
        b.wait(sy, "ve", std)
        if mode == "kv":
            b.dma(sy, g["K_hbm"][:, :, t0:t0 + w].rearrange("h d q -> d h q"),
                  p["Kst"].rearrange("p (h q) -> p h q", q=CH)[:, :, 0:w])
            war.set("Kst", "dsy")
        else:
            b.dma(sy, g["Q_hbm"][:, :, t0 - a:t0 - a + w].rearrange("h d q -> d h q"),
                  p["Qst"].rearrange("p (h q) -> p h q", q=CH)[:, :, 0:w])
            war.set("Qst", "dsy")
            b.dma(sy, g["xnown"][:, t0 - a:t0 - a + w].rearrange("(mc p) q -> p mc q", p=128),
                  p["xnst"].rearrange("p (mc q) -> p mc q", mc=16)[:, :, 0:w])
            war.set("xnst", "dsy")
        war.set("rc", "ve")
        war.set("rs", "ve")


def emit_routing_tile(b, g, p, pers, gidx, war):
    pe, ve, sc = b.pe, b.ve, b.sc
    xfd0 = b.tok("ve")
    for half in range(2):
        b.wait(pe, "ve", xfd0)
        war.guard(pe, "pA")
        for mc in range(8):
            c = half * 8 + mc
            ins = pe.transpose(p["pA"].bitcast(F32)[:, mc * 128:(mc + 1) * 128],
                               p["xn_f"][:, c * 128:(c + 1) * 128], pers["idnf"][:])
        tr = b.bump(ins, "pe")
        b.wait(ve, "pe", tr)
        war.guard(ve, "xnf_fm")
        b.chain(ve.tensor_copy(p["xnf_fm"][:, half * 1024:(half + 1) * 1024],
                               p["pA"][:, 0:1024]), "ve")
        war.set("pA", "ve")
    war.set("xn_f", "pe")
    xfd = b.tok("ve")
    b.wait(pe, "ve", xfd)
    war.guard(pe, "psg")
    for mc in range(16):
        ins = pe.matmul(p["psg"][:], p["xnf_fm"][:, mc * 128:(mc + 1) * 128],
                        pers["gateT"][:, mc * NE:(mc + 1) * NE],
                        start=(mc == 0), stop=(mc == 15))
    gmt = b.bump(ins, "pe")
    war.set("xnf_fm", "pe", gmt)
    b.wait(sc, "pe", gmt)
    sgd = b.bump(sc.activation(p["scores"][:], p["psg"][:], AF.Sigmoid), "sc")
    war.set("psg", "sc", sgd)
    sgd = b.fence("sc")
    b.wait(ve, "sc", sgd)
    b.chain(ve.tensor_add(p["vv"][:], p["scores"][:], pers["biasrep"][:]), "ve")
    b.chain(ve.reduce_max(p["r1"][:], p["vv"][:], axis=AX.X), "ve")
    b.chain(ve.tensor_scalar(p["m1m"][:], p["vv"][:], p["r1"][:], None, ALU.is_ge), "ve")
    b.chain(ve.scalar_tensor_tensor(p["v2"][:], p["m1m"][:], -1e30, p["vv"][:],
                                    ALU.mult, ALU.add), "ve")
    b.chain(ve.reduce_max(p["r2"][:], p["v2"][:], axis=AX.X), "ve")
    b.chain(ve.tensor_scalar(p["m2m"][:], p["v2"][:], p["r2"][:], None, ALU.is_ge), "ve")
    b.chain(ve.tensor_add(p["maskt"][:], p["m1m"][:], p["m2m"][:]), "ve")
    b.bump(ve.tensor_tensor(pers["s_mine"][:, gidx * NE:(gidx + 1) * NE],
                            p["scores"][:], p["maskt"][:], ALU.mult), "ve")


def emit_attention(b, g, p, pers, a, bb):
    pe, ve, sc, sy, gp = b.pe, b.ve, b.sc, b.sy, b.gp
    ntok = bb - a
    nkt = bb // 128
    SEG = 16
    war = War(b)

    pf_tok = None
    for h in range(NH):
        hp = h % 2
        if h == 0:
            war.guard(sy, ("K", hp))
            b.dma(sy, p["K"][hp][:, 0:bb], g["K_hbm"][h, :, 0:bb])
            lv = b.dma(sy, p["V"][hp].rearrange("p (tb e) -> p tb e", e=129)[:, 0:nkt, 0:128],
                       g["V_hbm"][h, 0:bb, :].rearrange("(tb p) d -> p tb d", p=128))
            b.wait(pe, "dsy", lv)
        else:
            b.wait(pe, "dgp", pf_tok)
        if h + 1 < NH:
            np_ = (h + 1) % 2
            war.guard(gp, ("K", np_))
            b.dma(gp, p["K"][np_][:, 0:bb], g["K_hbm"][h + 1, :, 0:bb], "dgp")
            pf_tok = b.dma(gp, p["V"][np_].rearrange("p (tb e) -> p tb e", e=129)[:, 0:nkt, 0:128],
                           g["V_hbm"][h + 1, 0:bb, :].rearrange("(tb p) d -> p tb d", p=128), "dgp")
        ohd = 0
        prev_p2ve = 0
        for qs, qw in _chunks(0, ntok, 512):
            qsg = a + qs
            qp = (qs // 512) % 2
            war.guard(sy, ("Q", qp))
            b.wait(sy, "ve", prev_p2ve)
            lq = b.dma(sy, p["Q"][qp][:, 0:qw], g["Q_hbm"][h, :, qs:qs + qw])
            b.wait(pe, "dsy", lq)
            ntile = qw // 128
            nkc = (qsg + qw) // 128
            nseg = (nkc + SEG - 1) // SEG
            if DBG and h == 0 and qs == 0:
                b.dma(sy, g["KSB"][:, 0:bb], p["K"][hp][:, 0:bb], "dbgs")
                b.dma(sy, g["VSB"][:, 0:nkt * 129], p["V"][hp][:, 0:nkt * 129], "dbgs")
            for sg in range(nseg):
                sp = sg % 2
                kc0, kc1 = sg * SEG, min((sg + 1) * SEG, nkc)
                pw3 = p["Pw"][sp].rearrange("p (k q) -> p k q", q=512)
                for kc in range(kc0, kc1):
                    kg = kc * 128
                    off = max(0, kg - qsg)
                    if off >= qw:
                        continue
                    sw = qw - off
                    spp = kc % 2
                    war.guard(pe, ("S", spp))
                    smm = b.bump(pe.matmul(p["S"][spp][:, 0:sw], p["K"][hp][:, kg:kg + 128],
                                           p["Q"][qp][:, off:off + sw],
                                           start=True, stop=True), "pe")
                    if kg >= qsg:
                        b.wait(ve, "pe", smm)
                        vmm = b.bump(ve.tensor_add(p["S"][spp][:, 0:128],
                                                   p["S"][spp][:, 0:128], pers["tri"][:]), "ve")
                        b.wait(sc, "ve", vmm)
                    else:
                        b.wait(sc, "pe", smm)
                    war.guard(sc, ("Pps", spp))
                    ex = b.bump(sc.activation(p["Pps"][spp][:, 0:sw],
                                              p["S"][spp][:, 0:sw], AF.Exp), "sc")
                    war.set(("S", spp), "sc", ex)
                    b.wait(ve, "sc", ex)
                    war.guard(ve, ("Pw", sp))
                    exv = b.chain(ve.tensor_copy(pw3[:, kc - kc0, 0:sw],
                                                 p["Pps"][spp][:, 0:sw]), "ve")
                    war.set(("Pps", spp), "ve", exv)
                exd = b.tok("ve")
                if DBG and h == 1 and qs == 0 and sg == 0:
                    b.wait(sy, "ve", exd)
                    b.dma(sy, g["PWD"], p["Pw"][sp][:], "dbgs")
                b.wait(pe, "ve", exd)
                for tt in range(ntile):
                    toff = tt * 128
                    last_kc = (qsg + toff) // 128
                    if last_kc < kc0:
                        continue
                    war.guard(pe, ("opsum", tt % 2))
                    ins = None
                    for kc in range(kc0, min(last_kc, kc1 - 1) + 1):
                        off = max(0, kc * 128 - qsg)
                        pcol = toff - off
                        ins = pe.matmul(p["oaug"][tt % 2][:, 0:129],
                                        pw3[:, kc - kc0, pcol:pcol + 128],
                                        p["V"][hp][:, kc * 129:kc * 129 + 129],
                                        start=(kc == kc0), stop=(kc == min(last_kc, kc1 - 1)))
                    pvt = b.bump(ins, "pe")
                    b.wait(ve, "pe", pvt)
                    oa = p["oacc"][:, tt * 129:(tt + 1) * 129]
                    if sg == 0:
                        war.guard(ve, ("oacc", tt))
                        cpd = b.chain(ve.tensor_copy(oa, p["oaug"][tt % 2][:]), "ve")
                    else:
                        cpd = b.chain(ve.tensor_tensor(oa, oa, p["oaug"][tt % 2][:], ALU.add), "ve")
                    war.set(("opsum", tt % 2), "ve", cpd)
                war.set(("Pw", sp), "pe")
            prev_p2ve = b.tok("ve")
            if DBG and h == 1 and qs == 0:
                b.wait(sy, "ve", b.tok("ve"))
                b.dma(sy, g["OACCD"], p["oacc"][:], "dbgs")
            for tt in range(ntile):
                oa = p["oacc"][:, tt * 129:(tt + 1) * 129]
                rsd = b.chain(ve.reciprocal(p["rs"][:], oa[:, 128:129]), "ve")
                war.guard(ve, "obf")
                ob = b.chain(ve.tensor_scalar(p["obf"][:], oa[:, 0:128], p["rs"][:], None,
                                              ALU.mult), "ve")
                war.set(("oacc", tt), "ve", ob)
                b.wait(pe, "ve", ob)
                war.guard(pe, "otp")
                otr = b.bump(pe.transpose(p["otp"][:, tt * 128:(tt + 1) * 128],
                                          p["obf"][:], pers["idn"][:]), "pe")
                war.set("obf", "pe", otr)
            trd = b.tok("pe")
            b.wait(ve, "pe", trd)
            war.guard(ve, "ohead")
            ohd = b.bump(ve.tensor_copy(p["ohead"][:, qs:qs + qw], p["otp"][:, 0:qw]), "ve")
            war.set("otp", "ve", ohd)
        b.wait(sy, "ve", ohd)
        if DBG and h == 1:
            b.dma(sy, g["OHD"][:, 0:ntok], p["ohead"][:, 0:ntok], "dbgs")
        b.dma(sy, g["o_hbm"][h, :, 0:ntok], p["ohead"][:, 0:ntok])
        war.set("ohead", "dsy")
        war.set(("K", hp), "pe")
        for qp_ in range(2):
            war.set(("Q", qp_), "pe")



def emit_ffn(b, g, p, pers, a, bb):
    pe, ve, sc, sy, gp = b.pe, b.ve, b.sc, b.sy, b.gp
    war = War(b)

    for s0, swid in _chunks(a, bb, SUB):
        nt = swid // 128
        soff = s0 - a
        war.guard(gp, "wo")
        lwo = b.dma(gp, p["wo"].rearrange("p (mc n) -> p mc n", mc=16),
                    g["wo2"].rearrange("(mc p) n -> p mc n", p=128), "dgp")
        war.guard(sy, "xns")
        lxn = b.dma(sy, p["xns"].rearrange("p (mc q) -> p mc q", mc=16)[:, :, 0:swid],
                    g["xnown"][:, soff:soff + swid].rearrange("(mc p) q -> p mc q", p=128))
        b.wait(pe, "dgp", lwo)
        b.wait(pe, "dsy", lxn)
        for tt in range(nt):
            otp = tt % 2
            war.guard(sy, ("ot", otp))
            lo = b.dma(sy, p["ot"][otp].rearrange("p (h q) -> p h q", h=NH),
                       g["o_hbm"][:, :, soff + tt * 128: soff + (tt + 1) * 128]
                       .rearrange("h d q -> d h q"))
            b.wait(pe, "dsy", lo)
            war.guard(pe, "ps2")
            for hc in range(NH):
                for j in range(4):
                    ins = pe.matmul(p["ps2"][:, j * 512:(j + 1) * 512],
                                    p["ot"][otp][:, hc * 128:(hc + 1) * 128],
                                    p["wo"][:, hc * 2048 + j * 512: hc * 2048 + (j + 1) * 512],
                                    start=(hc == 0), stop=(hc == NH - 1))
            opd = b.bump(ins, "pe")
            war.set(("ot", otp), "pe", opd)
            b.wait(ve, "pe", opd)
            b.bump(ve.tensor_copy(p["acc"][:, tt * 2048:(tt + 1) * 2048], p["ps2"][:]), "ve")
            war.set("ps2", "ve")
        war.set("wo", "pe")
        war.set(("w13", 0), "pe")
        war.set(("w13", 1), "pe")
        for e in range(NE + 1):
            shared = (e == NE)
            for hh in range(2):
                wp = (e * 2 + hh) % 2
                war.guard(gp, ("w13", wp))
                if shared:
                    s1ap = g["sw1T"][:, hh * 512:(hh + 1) * 512]
                    s3ap = g["sw3T"][:, hh * 512:(hh + 1) * 512]
                    s2ap = g["sw2T"][hh * 512:(hh + 1) * 512, :]
                else:
                    s1ap = g["w1T"][e, :, hh * 512:(hh + 1) * 512]
                    s3ap = g["w3T"][e, :, hh * 512:(hh + 1) * 512]
                    s2ap = g["w2T"][e, hh * 512:(hh + 1) * 512, :]
                b.dma(gp, p["w13"][wp].rearrange("p (mc z) -> p mc z", z=1024)[:, :, 0:512],
                      s1ap.rearrange("(mc p) z -> p mc z", p=128), "dgp")
                b.dma(gp, p["w13"][wp].rearrange("p (mc z) -> p mc z", z=1024)[:, :, 512:1024],
                      s3ap.rearrange("(mc p) z -> p mc z", p=128), "dgp")
                lw = b.dma(gp, p["w2"][wp].rearrange("p (hc m) -> p hc m", hc=4),
                           s2ap.rearrange("(hc p) m -> p hc m", p=128), "dgp")
                b.wait(pe, "dgp", lw)
                for tt in range(nt):
                    gidx = soff // 128 + tt
                    scl = None if shared else pers["s_mine"][:, gidx * NE + e: gidx * NE + e + 1]
                    war.guard(pe, "ps1")
                    for mc in range(16):
                        pe.matmul(p["ps1"][:, 0:512],
                                  p["xns"][:, mc * SUB + tt * 128: mc * SUB + (tt + 1) * 128],
                                  p["w13"][wp][:, mc * 1024: mc * 1024 + 512],
                                  start=(mc == 0), stop=(mc == 15))
                        ins = pe.matmul(p["ps1"][:, 512:1024],
                                        p["xns"][:, mc * SUB + tt * 128: mc * SUB + (tt + 1) * 128],
                                        p["w13"][wp][:, mc * 1024 + 512: mc * 1024 + 1024],
                                        start=(mc == 0), stop=(mc == 15))
                    m1d = b.bump(ins, "pe")
                    b.wait(sc, "pe", m1d)
                    war.guard(sc, "tsb")
                    if shared:
                        sil = b.bump(sc.activation(p["tsb"][:], p["ps1"][:, 0:512], AF.Silu), "sc")
                    else:
                        sil = b.bump(sc.activation(p["tsb"][:], p["ps1"][:, 0:512], AF.Silu,
                                                   scale=scl), "sc")
                    sil = b.fence("sc")
                    b.wait(ve, "sc", sil)
                    war.guard(ve, "hsb")
                    if shared:
                        hbd = b.chain(ve.tensor_tensor(p["hsb"][:], p["tsb"][:],
                                                       p["ps1"][:, 512:1024], ALU.mult), "ve")
                    else:
                        hbd = b.chain(ve.scalar_tensor_tensor(
                            p["hsb"][:], p["tsb"][:], scl,
                            p["ps1"][:, 512:1024], ALU.mult, ALU.mult), "ve")
                    war.set("ps1", "ve", hbd)
                    war.set("tsb", "ve", hbd)
                    b.wait(pe, "ve", hbd)
                    war.guard(pe, "htp")
                    for hc in range(4):
                        ins = pe.transpose(p["htp"][:, hc * 128:(hc + 1) * 128],
                                           p["hsb"][:, hc * 128:(hc + 1) * 128], pers["idn"][:])
                    htt = b.bump(ins, "pe")
                    war.set("hsb", "pe", htt)
                    b.wait(ve, "pe", htt)
                    war.guard(ve, "hT")
                    hte = b.chain(ve.tensor_copy(p["hT"][:], p["htp"][:]), "ve")
                    war.set("htp", "ve", hte)
                    b.wait(pe, "ve", hte)
                    war.guard(pe, "ps2")
                    for hc in range(4):
                        for j in range(4):
                            ins = pe.matmul(p["ps2"][:, j * 512:(j + 1) * 512],
                                            p["hT"][:, hc * 128:(hc + 1) * 128],
                                            p["w2"][wp][:, hc * 2048 + j * 512: hc * 2048 + (j + 1) * 512],
                                            start=(hc == 0), stop=(hc == 3))
                    m2d = b.bump(ins, "pe")
                    war.set("hT", "pe", m2d)
                    b.wait(sc, "pe", m2d)
                    war.guard(sc, "scr")
                    scd = b.bump(sc.activation(p["scr"][:], p["ps2"][:], AF.Copy), "sc")
                    war.set("ps2", "sc", scd)
                    scd = b.fence("sc")
                    b.wait(gp, "sc", scd)
                    gad = b.chain(gp.tensor_tensor(p["acc"][:, tt * 2048:(tt + 1) * 2048],
                                                   p["acc"][:, tt * 2048:(tt + 1) * 2048],
                                                   p["scr"][:], ALU.add), "gp")
                    war.set("scr", "gp", gad)
                war.set(("w13", wp), "pe")
        accd = b.tok("gp")
        for tt in range(nt):
            xp = tt % 2
            war.guard(sy, ("xr", xp))
            lx = b.dma(sy, p["xr"][xp][:], g["x2d"][s0 + tt * 128: s0 + (tt + 1) * 128, :])
            b.wait(ve, "dsy", lx)
            b.wait(ve, "gp", accd)
            war.guard(ve, ("xo", xp))
            od = b.bump(ve.tensor_add(p["xo"][xp][:], p["acc"][:, tt * 2048:(tt + 1) * 2048],
                                      p["xr"][xp][:]), "ve")
            war.set(("xr", xp), "ve", od)
            b.wait(sy, "ve", od)
            b.dma(sy, g["OUT"][soff + tt * 128: soff + (tt + 1) * 128, :], p["xo"][xp][:])
            war.set(("xo", xp), "dsy")
        war.set("acc", "ve")
        war.set("xns", "pe")
    b.wait(sy, "dsy", b.tok("dsy"))


def emit_phase(b, name, alloc_fn, common_fn, body_fn):
    nc = b.nc
    with ExitStack() as pctx:
        p = alloc_fn(pctx)
        if b.nphase > 0:
            with nc.Block(f"{name}_bar") as blk:
                phv = 5 * b.nphase
                for attr in ("tensor", "vector", "scalar", "gpsimd", "sync"):
                    def _mk(eng_attr):
                        def _f(eng):
                            eng.wait_ge(b.sem["phase"], phv)
                        return _f
                    getattr(blk, attr)(_mk(attr))
        if common_fn is not None:
            common_fn(p)
        if MODE_M1:
            b.rank = None
            body_fn(p, 0)
            b.wait(b.sy, "dsy", b.tok("dsy"))
            b.wait(b.gp, "dgp", b.tok("dgp"))
            for eng in b.engs:
                b.bump(eng.nop(), "phase")
            b.nphase += 1
            return
        for arm in nc.switch_core_id(b.engs, n=128):
            r = arm.logical % 8
            for eng in b.engs:
                eng.br(f"{name}_body{r}")
        for r in range(NCORE):
            with nc.body(f"{name}_body{r}", valid_engines=ALL_EN):
                b.rank = r
                body_fn(p, r)
                b.wait(b.sy, "dsy", b.tok("dsy"))
                b.wait(b.gp, "dgp", b.tok("dgp"))
                for eng in b.engs:
                    b.bump(eng.nop(), "phase")
                for eng in b.engs:
                    eng.br(f"{name}_join")
            b.rank = None
        nc.switch_bb(f"{name}_join")
        b.nphase += 1


def build_nc(nmax, bounds):
    nc = bass.Bass("TRN2", target_bir_lowering=False, debug=False, num_devices=NCORE)
    g = declare_io(nc, nmax)
    ctx = ExitStack()
    with ctx:
        b = BState(nc, ctx)
        pers = {}
        maxg = nmax // 128
        pers["idn"] = ctx.enter_context(nc.sbuf_tensor("idn_s", [128, 128], BF16))
        pers["idnf"] = ctx.enter_context(nc.sbuf_tensor("idnf_s", [128, 128], F32))
        pers["tri"] = ctx.enter_context(nc.sbuf_tensor("tri_s", [128, 128], F32))
        pers["gateT"] = ctx.enter_context(nc.sbuf_tensor("gateTs", [128, 16 * NE], F32))
        pers["biasrep"] = ctx.enter_context(nc.sbuf_tensor("biasreps", [128, NE], F32))
        pers["s_mine"] = ctx.enter_context(nc.sbuf_tensor("s_mine", [128, maxg * NE], F32))
        pers["epst"] = ctx.enter_context(nc.sbuf_tensor("epst", [128, 1], F32))
        pers["acttmp"] = ctx.enter_context(nc.sbuf_tensor("acttmp", [128, 1], F32))

        with nc.Block("prol") as blk:
            @blk.sync
            def _(sy):
                b.dma(sy, pers["idn"][:], g["idn"])
                b.dma(sy, pers["idnf"][:], g["idnf"])
                b.dma(sy, pers["tri"][:], g["tri"])
                b.dma(sy, pers["gateT"].rearrange("p (mc e) -> p mc e", e=NE),
                      g["gateT"].rearrange("(mc p) e -> p mc e", p=128))
                b.dma(sy, pers["biasrep"][:], g["biasrep"])
            dtok = b.tok("dsy")
            @blk.tensor
            def _(te):
                b.wait(te, "dsy", dtok)
            @blk.vector
            def _(ve):
                b.wait(ve, "dsy", dtok)
                b.bump(ve.memset(pers["epst"][:], 1e-5), "ve")
            vtok0 = b.tok("ve")
            @blk.scalar
            def _(sc):
                b.wait(sc, "dsy", dtok)
                b.wait(sc, "ve", vtok0)
            @blk.gpsimd
            def _(gp_):
                b.wait(gp_, "dsy", dtok)

        def alloc_p1(pctx):
            e = pctx.enter_context
            p = {}
            p["wA"] = e(nc.sbuf_tensor("wk_sb", [128, 16 * 2048], BF16))
            p["wB"] = e(nc.sbuf_tensor("wv_sb", [128, 16 * 2048], BF16))
            p["x"] = e(nc.sbuf_tensor("x_sb", [128, 2048], F32))
            p["rc"] = e(nc.sbuf_tensor("rc_sb", [128, 4 * 64], F32))
            p["rs"] = e(nc.sbuf_tensor("rs_sb", [128, 4 * 64], F32))
            p["xn_fm"] = e(nc.sbuf_tensor("xn_fm", [128, 2048], BF16))
            p["xn_bf"] = e(nc.sbuf_tensor("xn_bf", [128, 2048], BF16))
            p["ktm"] = e(nc.sbuf_tensor("ktm", [128, 2048], BF16))
            p["vtm"] = e(nc.sbuf_tensor("vtm", [128, 2048], BF16))
            p["Kst"] = e(nc.sbuf_tensor("Kst", [128, NH * CH], BF16))
            p["sq"] = e(nc.sbuf_tensor("sq", [128, 2048], F32))
            for nm in ("msum", "sqs", "mean", "sqm", "m2", "var", "sd", "rstd"):
                p[nm] = e(nc.sbuf_tensor(nm, [128, 1], F32))
            p["pA"] = e(nc.psum_tensor("pA", [128, 2048], F32))
            p["pT"] = e(nc.psum_tensor("pT", [128, 2048], BF16))
            return p

        def common_p1(p):
            with nc.Block("ldkv") as blk:
                @blk.scalar
                def _(sc_):
                    sc_.activation(pers["acttmp"][:], pers["epst"][:], AF.Sqrt)
                @blk.gpsimd
                def _(gp_):
                    b.dma(gp_, p["wA"].rearrange("p (mc n) -> p mc n", mc=16),
                          g["wk2"].rearrange("(mc p) n -> p mc n", p=128), "dgp")
                    b.dma(gp_, p["wB"].rearrange("p (mc n) -> p mc n", mc=16),
                          g["wv2"].rearrange("(mc p) n -> p mc n", p=128), "dgp")
                dtok = b.tok("dgp")
                @blk.tensor
                def _(te):
                    b.wait(te, "dgp", dtok)

        def body_p1(p, r):
            if ONLY_PHASES is None or "p1" in ONLY_PHASES:
                emit_ctx_pass(b, g, p, pers, 0, bounds[r], "kv")

        emit_phase(b, "p1", alloc_p1, common_p1, body_p1)

        def alloc_p15(pctx):
            e = pctx.enter_context
            p = {}
            p["wA"] = e(nc.sbuf_tensor("b_wq_sb", [128, 16 * 2048], BF16))
            p["x"] = e(nc.sbuf_tensor("b_x_sb", [128, 2048], F32))
            p["rc"] = e(nc.sbuf_tensor("b_rc_sb", [128, 4 * 64], F32))
            p["rs"] = e(nc.sbuf_tensor("b_rs_sb", [128, 4 * 64], F32))
            p["xnst"] = e(nc.sbuf_tensor("b_xnst", [128, 16 * CH], BF16))
            p["xn_bf"] = e(nc.sbuf_tensor("b_xn_bf", [128, 2048], BF16))
            p["xn_f"] = e(nc.sbuf_tensor("b_xn_f", [128, 2048], F32))
            p["ktm"] = e(nc.sbuf_tensor("b_ktm", [128, 2048], BF16))
            p["Qst"] = e(nc.sbuf_tensor("b_Qst", [128, NH * CH], BF16))
            p["sq"] = e(nc.sbuf_tensor("b_sq", [128, 2048], F32))
            p["xnf_fm"] = e(nc.sbuf_tensor("b_xnf_fm", [128, 2048], F32))
            for nm in ("msum", "sqs", "mean", "sqm", "m2", "var", "sd", "rstd"):
                p[nm] = e(nc.sbuf_tensor("b_" + nm, [128, 1], F32))
            for nm in ("scores", "vv", "m1m", "v2", "m2m", "maskt"):
                p[nm] = e(nc.sbuf_tensor("b_" + nm, [128, NE], F32))
            for nm in ("r1", "r2"):
                p[nm] = e(nc.sbuf_tensor("b_" + nm, [128, 1], F32))
            p["pA"] = e(nc.psum_tensor("b_pA", [128, 2048], F32))
            p["pT"] = e(nc.psum_tensor("b_pT", [128, 2048], BF16))
            p["psg"] = e(nc.psum_tensor("b_psg", [128, NE], F32))
            return p

        def common_p15(p):
            with nc.Block("ldq") as blk:
                phv = 5 * b.nphase
                @blk.scalar
                def _(sc_):
                    sc_.wait_ge(b.sem["phase"], phv)
                    sc_.activation(pers["acttmp"][:], pers["epst"][:], AF.Sigmoid)
                @blk.gpsimd
                def _(gp_):
                    gp_.wait_ge(b.sem["phase"], phv)
                    b.dma(gp_, p["wA"].rearrange("p (mc n) -> p mc n", mc=16),
                          g["wq2"].rearrange("(mc p) n -> p mc n", p=128), "dgp")
                dtok = b.tok("dgp")
                @blk.tensor
                def _(te):
                    b.wait(te, "dgp", dtok)

        def body_p15(p, r):
            if ONLY_PHASES is None or "p15" in ONLY_PHASES:
                a = 0 if r == 0 else bounds[r - 1]
                emit_ctx_pass(b, g, p, pers, a, bounds[r], "q", a=a)

        emit_phase(b, "p15", alloc_p15, common_p15, body_p15)

        def alloc_p2(pctx):
            e = pctx.enter_context
            p = {}
            nkt_max = bounds[-1] // 128
            p["K"] = [e(nc.sbuf_tensor(f"c_K{i}", [128, bounds[-1]], BF16)) for i in range(2)]
            p["V"] = [e(nc.sbuf_tensor(f"c_V{i}", [128, nkt_max * 129], BF16)) for i in range(2)]
            p["Q"] = [e(nc.sbuf_tensor(f"c_Q{i}", [128, 512], BF16)) for i in range(2)]
            p["Pw"] = [e(nc.sbuf_tensor(f"c_Pw{i}", [128, 16 * 512], BF16)) for i in range(2)]
            p["oacc"] = e(nc.sbuf_tensor("c_oacc", [128, 4 * 129], F32))
            p["ohead"] = e(nc.sbuf_tensor("c_ohead", [128, nmax], BF16))
            p["obf"] = e(nc.sbuf_tensor("c_obf", [128, 128], BF16))
            p["rs"] = e(nc.sbuf_tensor("c_rs", [128, 1], F32))
            p["S"] = [e(nc.psum_tensor(f"c_S{i}", [128, 512], F32)) for i in range(2)]
            p["Pps"] = [e(nc.psum_tensor(f"c_Pps{i}", [128, 512], F32)) for i in range(2)]
            p["oaug"] = [e(nc.psum_tensor(f"c_oaug{i}", [128, 129], F32)) for i in range(2)]
            p["otp"] = e(nc.psum_tensor("c_otp", [128, 512], BF16))
            return p

        def common_p2(p):
            with nc.Block("p2init") as blk:
                phv = 5 * b.nphase
                @blk.scalar
                def _(sc_):
                    sc_.wait_ge(b.sem["phase"], phv)
                    sc_.activation(pers["acttmp"][:], pers["epst"][:], AF.Exp)
                @blk.vector
                def _(ve):
                    ve.wait_ge(b.sem["phase"], phv)
                    for i in range(2):
                        b.bump(ve.memset(
                            p["V"][i].rearrange("p (tb e) -> p tb e", e=129)[:, :, 128:129],
                            1.0), "ve")
                vtok = b.tok("ve")
                @blk.sync
                def _(sy):
                    b.wait(sy, "ve", vtok)

        def body_p2(p, r):
            if ONLY_PHASES is None or "p2" in ONLY_PHASES:
                a = 0 if r == 0 else bounds[r - 1]
                emit_attention(b, g, p, pers, a, bounds[r])

        emit_phase(b, "p2", alloc_p2, common_p2, body_p2)

        def alloc_p3(pctx):
            e = pctx.enter_context
            p = {}
            arena = e(nc.sbuf_tensor("d_war3", [128, 49152], BF16))
            p["wo"] = arena[:, 0:32768]
            p["w13"] = [arena[:, 0:16384], arena[:, 16384:32768]]
            p["w2"] = [arena[:, 32768:40960], arena[:, 40960:49152]]
            p["ot"] = [e(nc.sbuf_tensor(f"d_ot{i}", [128, NH * 128], BF16)) for i in range(2)]
            p["xns"] = e(nc.sbuf_tensor("d_xns", [128, 16 * SUB], BF16))
            p["acc"] = e(nc.sbuf_tensor("d_acc", [128, (SUB // 128) * 2048], F32))
            p["scr"] = e(nc.sbuf_tensor("d_scr", [128, 2048], F32))
            p["tsb"] = e(nc.sbuf_tensor("d_tsb", [128, 512], F32))
            p["hsb"] = e(nc.sbuf_tensor("d_hsb", [128, 512], BF16))
            p["hT"] = e(nc.sbuf_tensor("d_hT", [128, 512], BF16))
            p["xr"] = [e(nc.sbuf_tensor(f"d_xr{i}", [128, 2048], F32)) for i in range(2)]
            p["xo"] = [e(nc.sbuf_tensor(f"d_xo{i}", [128, 2048], F32)) for i in range(2)]
            p["ps1"] = e(nc.psum_tensor("d_ps1", [128, 1024], F32))
            p["ps2"] = e(nc.psum_tensor("d_ps2", [128, 2048], F32))
            p["htp"] = e(nc.psum_tensor("d_htp", [128, 512], BF16))
            return p

        def common_p3(p):
            with nc.Block("p3w") as blk:
                phv = 5 * b.nphase
                @blk.scalar
                def _(sc_):
                    sc_.wait_ge(b.sem["phase"], phv)
                    sc_.activation(pers["acttmp"][:], pers["epst"][:], AF.Silu)
                @blk.gpsimd
                def _(gp_):
                    gp_.wait_ge(b.sem["phase"], phv)

        def body_p3(p, r):
            if ONLY_PHASES is None or "p3" in ONLY_PHASES:
                a = 0 if r == 0 else bounds[r - 1]
                emit_ffn(b, g, p, pers, a, bounds[r])

        emit_phase(b, "p3", alloc_p3, common_p3, body_p3)
        if not MODE_M1:
            nc.switch_bb("final_bb")
        if DBG:
            with nc.Block("dbg") as blk:
                phv = 5 * b.nphase
                @blk.gpsimd
                def _(gp_):
                    gp_.wait_ge(b.sem["phase"], phv)
                    b.dma(gp_, g["KDBG"], g["K_hbm"], "dbgs")
                    b.dma(gp_, g["VDBG"], g["V_hbm"], "dbgs")
                    b.dma(gp_, g["QDBG"], g["Q_hbm"], "dbgs")
                    b.dma(gp_, g["ODBG"], g["o_hbm"], "dbgs")
                    b.dma(gp_, g["XNDBG"], g["xnown"], "dbgs")
                    b.dma(gp_, g["SDBG"][:, 0:pers["s_mine"].shape[1]], pers["s_mine"][:], "dbgs")
                    gp_.wait_ge(b.sem["dbgs"], b.tok("dbgs"))
    return nc


def kernel(**inputs):
    hi = host_inputs(inputs)
    nmax = T if MODE_M1 else NMAX
    bounds = [T] * NCORE if MODE_M1 else BOUNDS
    nc = build_nc(nmax, bounds)
    in_maps = [dict(hi) for _ in range(NCORE)]
    res = run_bass_kernel_spmd(nc, in_maps, core_ids=list(range(NCORE)))
    if MODE_M1:
        out = res.results[0]["OUT"]
    else:
        parts = []
        a = 0
        for r in range(NCORE):
            parts.append(res.results[r]["OUT"][0:bounds[r] - a, :])
            a = bounds[r]
        out = np.concatenate(parts, axis=0)
    return out.reshape(1, T, M).astype(np.float32)
